# version 06
# baseline (speedup 1.0000x reference)
import math
from collections import defaultdict
from contextlib import ExitStack

import numpy as np
import ml_dtypes

import concourse.bass as bass
import concourse.mybir as mybir
from concourse.bass_utils import run_bass_kernel_spmd

F32 = mybir.dt.float32
BF16 = mybir.dt.bfloat16
AF = mybir.ActivationFunctionType
ALU = mybir.AluOpType
AXX = mybir.AxisListType.X
bfdt = ml_dtypes.bfloat16

B, L, H, II = 8, 4096, 1024, 2816
NCORES = 8
LNG = float(np.float32(math.log(0.9)))
EPS = 1e-5
NCH = 8
QB = 256
NQB = L // QB
NKT = L // 128
TH2 = [0.0625, 0.5625, 1.5625, 3.0625, 6.25, 12.25, 25.0]

ARB_N = 80896
ARF_N = 7168


def _mk_tables():
    i = np.arange(512, dtype=np.float64)
    scale_vec = (2.0 * i + 0.4 * H) / (1.4 * H)
    pos = np.arange(L, dtype=np.float64)
    scl = scale_vec[None, :] ** (pos / 512.0)[:, None]
    inv_freq = 1.0 / 10000.0 ** (i / 512.0)
    ang = pos[:, None] * inv_freq[None, :]
    c, s = np.cos(ang), np.sin(ang)
    return (
        (c * scl).astype(np.float32), (s * scl).astype(np.float32),
        (c / scl).astype(np.float32), (s / scl).astype(np.float32),
    )


class EM:
    ENGS = ("sync", "tensor", "vector", "scalar", "gpsimd")

    def __init__(self):
        self.q = {e: [] for e in self.ENGS}
        self.cnt = defaultdict(int)

    def on(self, eng, fn):
        self.q[eng].append(fn)

    def inc(self, eng, fn, sem, n=1):
        self.cnt[sem] += n
        self.q[eng].append(lambda e, fn=fn, sem=sem, n=n: fn(e).then_inc(sem, n))
        return self.cnt[sem]

    def wait(self, eng, sem, val):
        if val > 0:
            self.q[eng].append(lambda e, sem=sem, val=val: e.wait_ge(sem, val))

    def val(self, sem):
        return self.cnt[sem]


DEBUG_SCRATCH = False


def build_nc():
    nc = bass.Bass("TRN2", target_bir_lowering=False)
    _scr = "ExternalOutput" if DEBUG_SCRATCH else "Internal"

    X = nc.dram_tensor("x_in", [L, H], F32, kind="ExternalInput")
    WQ = nc.dram_tensor("wq_in", [H, H], F32, kind="ExternalInput")
    WK = nc.dram_tensor("wk_in", [H, H], F32, kind="ExternalInput")
    WV = nc.dram_tensor("wv_in", [H, H], F32, kind="ExternalInput")
    WG = nc.dram_tensor("wg_in", [II, H], F32, kind="ExternalInput")
    WU = nc.dram_tensor("wu_in", [II, H], F32, kind="ExternalInput")
    WD = nc.dram_tensor("wd_in", [H, II], F32, kind="ExternalInput")
    TCQ = nc.dram_tensor("tcq_in", [L, 512], F32, kind="ExternalInput")
    TSQ = nc.dram_tensor("tsq_in", [L, 512], F32, kind="ExternalInput")
    TCK = nc.dram_tensor("tck_in", [L, 512], F32, kind="ExternalInput")
    TSK = nc.dram_tensor("tsk_in", [L, 512], F32, kind="ExternalInput")
    IDB = nc.dram_tensor("idb_in", [128, 128], BF16, kind="ExternalInput")
    IDF = nc.dram_tensor("idf_in", [128, 128], F32, kind="ExternalInput")
    BIO = nc.dram_tensor("bio_in", [128, 512], F32, kind="ExternalInput")
    CST = nc.dram_tensor("cst_in", [128, 80], F32, kind="ExternalInput")
    OUT = nc.dram_tensor("out", [L, H], F32, kind="ExternalOutput")
    QF = nc.dram_tensor("qf_sc", [8, 128, L], BF16, kind=_scr)
    VS = nc.dram_tensor("vs_sc", [32, 128, H], BF16, kind=_scr)
    YS = nc.dram_tensor("ys_sc", [L, H], F32, kind=_scr)
    WGT = nc.dram_tensor("wgt_sc", [8, 128, II], BF16, kind=_scr)
    WUT = nc.dram_tensor("wut_sc", [8, 128, II], BF16, kind=_scr)
    WDT = nc.dram_tensor("wdt_sc", [22, 128, II // II * H], BF16, kind=_scr)
    DS = nc.dram_tensor("ds_sc", [128, 8192], BF16, kind=_scr) if DEBUG_SCRATCH else None
    DM = nc.dram_tensor("dm_sc", [128, 256], F32, kind=_scr) if DEBUG_SCRATCH else None
    DM2 = nc.dram_tensor("dm2_sc", [128, 2], F32, kind=_scr) if DEBUG_SCRATCH else None
    DM3 = nc.dram_tensor("dm3_sc", [1, 256], BF16, kind=_scr) if DEBUG_SCRATCH else None
    DM4 = nc.dram_tensor("dm4_sc", [128, 256], F32, kind=_scr) if DEBUG_SCRATCH else None
    DM5 = nc.dram_tensor("dm5_sc", [128, 8192], BF16, kind=_scr) if DEBUG_SCRATCH else None
    DM6 = nc.dram_tensor("dm6_sc", [128, 256], BF16, kind=_scr) if DEBUG_SCRATCH else None
    DM7 = nc.dram_tensor("dm7_sc", [1, 256], F32, kind=_scr) if DEBUG_SCRATCH else None
    DM8 = nc.dram_tensor("dm8_sc", [128, 2], F32, kind=_scr) if DEBUG_SCRATCH else None
    DM9 = nc.dram_tensor("dm9_sc", [128, 1024], F32, kind=_scr) if DEBUG_SCRATCH else None
    DM10 = nc.dram_tensor("dm10_sc", [128, 16], F32, kind=_scr) if DEBUG_SCRATCH else None
    DM11 = nc.dram_tensor("dm11_sc", [128, 1024], F32, kind=_scr) if DEBUG_SCRATCH else None
    DM12 = nc.dram_tensor("dm12_sc", [128, 16], F32, kind=_scr) if DEBUG_SCRATCH else None

    em = EM()

    with ExitStack() as ctx:
        def sb(name, shape, dt=F32):
            return ctx.enter_context(nc.sbuf_tensor(name, shape, dt))

        def sem(name):
            return ctx.enter_context(nc.semaphore(name))

        sIN = sem("sIN")
        sOUT = sem("sOUT")
        sTE = sem("sTE")
        sVE = sem("sVE")
        sAC = sem("sAC")
        sGP = sem("sGP")

        idb = sb("idb", [128, 128], BF16)
        idf = sb("idf", [128, 128], F32)
        bio = sb("bio", [128, 512], F32)
        onesb = sb("onesb", [1, 128], BF16)
        onesf = sb("onesf", [1, 128], F32)
        onesc = sb("onesc", [128, 1], BF16)
        cstb = sb("cstb", [128, 80], F32)
        ARB = sb("arb", [128, ARB_N], BF16)
        ARF = sb("arf", [128, ARF_N], F32)
        PF = ctx.enter_context(nc.psum_tensor("pallf", [128, 3584], F32))
        PB = ctx.enter_context(nc.psum_tensor("pallb", [128, 1024], BF16))

        def arb(o, n):
            assert o + n <= ARB_N
            return ARB[:, o:o + n]

        def arf(o, n):
            assert o + n <= ARF_N
            return ARF[:, o:o + n]

        kfm = arb(0, 32768)
        p1wq = arb(32768, 8192)
        p1wk = arb(40960, 8192)
        p1wv = arb(49152, 8192)
        xbf = arb(57344, 4096)
        xfm = arb(61440, 4096)
        qfmb = arb(65536, 4096)
        ptm = arb(69632, 1024)
        pxx = arb(70656, 1024)
        xin = arf(0, 4096)
        tt0 = [arf(4096, 512), arf(4608, 512)]
        tt1 = [arf(5120, 512), arf(5632, 512)]
        pxa = arf(6144, 512)
        pxb = arf(6656, 512)
        vtm = arb(32768, 32768)
        ssb = arb(65536, 8192)
        qblk = arb(73728, 2048)
        dtb = [arb(75776, 256), arb(76032, 256)]
        qbA = arb(76288, 512)
        qbB = arb(76800, 512)
        qbS1 = arb(77312, 512)
        qbS2 = arb(77824, 512)
        qbS3 = arb(78336, 512)
        qbSG = arb(78848, 512)
        wqo = arb(79360, 512)
        mq4b = arb(79872, 2)
        mrowB = arb(79876, 256)
        rzb = arb(80132, 256)
        uextra = arb(80388, 256)
        xres = arf(0, 1024)
        ypre = arf(1024, 1024)
        mxf = arf(2048, 256)
        mbc = arf(2304, 256)
        zrow = arf(2560, 256)
        rzf = arf(2816, 256)
        mq4 = arf(3072, 2)
        rzc = arf(3076, 2)
        stat = arf(3088, 16)
        ypc = arf(3104, 1024)
        qsrc = arf(5200, 512)
        qbm = arf(5712, 32)
        qsc = arf(5744, 32)
        qs4 = arf(5776, 32)
        qrc = arf(5808, 32)
        qrc2 = arf(5840, 32)
        qu = arf(5872, 512)
        qu2 = arf(6384, 512)
        wgTr = arb(0, 22528)
        wuTr = arb(22528, 22528)
        wdTr = arb(45056, 22528)
        yqb = arb(67584, 1024)
        yqT = arb(68608, 1024)
        hqT = arb(69632, 2816)
        qca = arb(72448, 1408)
        qcb = arb(73856, 1408)
        qs1 = arb(75264, 1408)
        qsg = arb(76672, 1408)
        qqo = arb(78080, 1408)
        qft = arb(79488, 1408)
        gs = arf(0, 2816)
        yti = arf(2816, 1024)
        ypre2 = arf(3840, 1024)
        st3 = arf(7136, 32)
        vbm = arf(6896, 32)
        vsc = arf(6928, 32)
        vs2 = arf(6960, 32)
        vrc = arf(6992, 32)
        vrc2 = arf(7024, 32)
        vA = arb(72448, 512)
        vB = arb(72960, 512)
        vS1 = arb(73472, 512)
        vS2 = arb(73984, 512)
        vS3 = arb(74496, 512)
        vSG = arb(75008, 512)
        pQ = PF[:, 0:1024]
        pS = [PF[:, 0:256], PF[:, 512:768]]
        pPV = PF[:, 1024:2048]
        pA4 = PF[:, 2048:2304]
        pA4b = PF[:1, 2304:2560]
        pRZ = PF[:, 2048:2050]
        pZ = PF[:1, 3072:3328]
        pG = [PF[:, 0:512], PF[:, 512:1024]]
        pVQ = PF[:, 2048:2560]
        pVU = PF[:, 2560:3072]
        pT = PB[:, 0:1024]
        pT4 = PB[:, 512:1024]
        pMR = PB[:1, 256:512]
        pMR2 = PB[:, 256:384]
        pRZB = PB[:, 256:260]

        def dma_in(src_ap, dst_ap):
            return em.inc("sync", lambda e, s=src_ap, d=dst_ap: e.dma_start(out=d, in_=s), sIN, 16)

        def dma_out(dst_ap, src_ap):
            return em.inc("scalar", lambda e, s=src_ap, d=dst_ap: e.dma_start(out=d, in_=s), sOUT, 16)

        def TT(*args):
            if len(args) == 5:
                args = args[1:]
            out, a, b, op = args
            return lambda e, out=out, a=a, b=b, op=op: e.tensor_tensor(out, a, b, op)

        def TS(out, a, s1, s2, op1, op2=None):
            if s2 is None:
                return lambda e, out=out, a=a, s1=s1, op1=op1: e.tensor_scalar(out, a, s1, None, op1)
            return lambda e, out=out, a=a, s1=s1, s2=s2, op1=op1, op2=op2: e.tensor_scalar(out, a, s1, s2, op1, op2)

        dma_in(IDB.ap(), idb[:, :])
        dma_in(IDF.ap(), idf[:, :])
        v_bio = dma_in(BIO.ap(), bio[:, :])
        dma_in(CST.ap(), cstb[:, :])
        em.on("vector", lambda e: e.memset(onesb[:1, :], 1.0))
        em.on("vector", lambda e: e.memset(onesf[:1, :], 1.0))
        em.on("vector", lambda e: e.memset(onesc[:, :1], 1.0))

        xin3 = xin.rearrange("p (t f) -> p t f", f=1024)
        wcast = 0
        for Wsrc, wdst in ((WQ, p1wq), (WK, p1wk), (WV, p1wv)):
            for g in range(2):
                em.wait("sync", sVE, wcast)
                v = dma_in(
                    bass.AP(Wsrc, g * 4 * 128 * 1024, [[1024, 128], [128 * 1024, 4], [1, 1024]]),
                    xin3,
                )
                em.wait("vector", sIN, v)
                wcast = em.inc(
                    "vector",
                    lambda e, g=g, wdst=wdst: e.tensor_copy(
                        wdst.rearrange("p (t f) -> p t f", f=1024)[:, 4 * g:4 * g + 4, :], xin3),
                    sVE,
                )

        def wsl(wbuf, hc, nch):
            return wbuf.rearrange("p (hc f) -> p hc f", f=1024)[:, hc, 512 * nch:512 * nch + 512]

        xfm3 = xfm.rearrange("p (hc f) -> p hc f", f=512)
        qfmb3 = qfmb.rearrange("p (hc f) -> p hc f", f=512)
        kfm3 = kfm.rearrange("p (hc f) -> p hc f", f=4096)
        pT3 = pT.rearrange("p (hc f) -> p hc f", f=128)
        pe = ptm.rearrange("p (k two) -> p k two", two=2)
        xe = pxx.rearrange("p (k two) -> p k two", two=2)

        xcast_done = wcast
        xt_evac = 0
        qk_evac = 0
        tab_free = {0: 0, 1: 0}
        spillq = 0
        last_mm = 0
        vspill = 0

        def proj_mms(wbuf, tt):
            r = 0
            for nch in range(2):
                for hc in range(8):
                    fn = lambda e, tt=tt, hc=hc, nch=nch, wbuf=wbuf: e.matmul(
                        pQ[:, 512 * nch:512 * nch + 512],
                        xfm[:, 512 * hc + 128 * tt: 512 * hc + 128 * tt + 128],
                        wsl(wbuf, hc, nch),
                        start=(hc == 0), stop=(hc == 7))
                    if nch == 1 and hc == 7:
                        r = em.inc("tensor", fn, sTE)
                    else:
                        em.on("tensor", fn)
            return r

        def xpos_emit(tq0, tq1):
            em.on("vector", TT("vector", pxa[:, :], pe[:, :, 0:1], tq0[:, :], ALU.mult))
            em.on("vector", TT("vector", pxb[:, :], pe[:, :, 1:2], tq1[:, :], ALU.mult))
            em.on("vector", TT("vector", xe[:, :, 0:1], pxa[:, :], pxb[:, :], ALU.subtract))
            em.on("vector", TT("vector", pxa[:, :], pe[:, :, 1:2], tq0[:, :], ALU.mult))
            em.on("vector", TT("vector", pxb[:, :], pe[:, :, 0:1], tq1[:, :], ALU.mult))
            return em.inc("vector", TT("vector", xe[:, :, 1:2], pxa[:, :], pxb[:, :], ALU.add), sVE)

        def tpose8(src, sem_=sTE):
            r = 0
            for hc in range(8):
                fn = lambda e, hc=hc, src=src: e.transpose(
                    pT[:, 128 * hc:128 * hc + 128], src[:, 128 * hc:128 * hc + 128], idb[:, :])
                if hc == 7:
                    r = em.inc("tensor", fn, sem_)
                else:
                    em.on("tensor", fn)
            return r

        for c in range(NCH):
            em.wait("sync", sVE, xcast_done)
            vx = dma_in(
                bass.AP(X, c * 512 * 1024, [[1024, 128], [128 * 1024, 4], [1, 1024]]), xin3)
            em.wait("vector", sIN, vx)
            em.wait("vector", sTE, last_mm)
            xcast_done = em.inc("vector", lambda e: e.tensor_copy(xbf[:, :], xin3), sVE)

            for tt in range(4):
                em.wait("tensor", sVE, xcast_done)
                em.wait("tensor", sVE, qk_evac)
                tp = tpose8(xbf[:, 1024 * tt:1024 * tt + 1024])
                em.wait("vector", sTE, tp)
                xt_evac = em.inc(
                    "vector",
                    lambda e, tt=tt: e.tensor_copy(xfm3[:, :, 128 * tt:128 * tt + 128], pT3),
                    sVE)

                par = tt % 2
                tq = tt0 if par == 0 else tt1
                row0 = c * 512 + tt * 128
                em.wait("sync", sTE, tab_free[par])
                dma_in(bass.AP(TCQ, row0 * 512, [[512, 128], [1, 512]]), tq[0][:, :])
                vtq = dma_in(bass.AP(TSQ, row0 * 512, [[512, 128], [1, 512]]), tq[1][:, :])

                em.wait("tensor", sVE, xt_evac)
                mmv = proj_mms(p1wv, tt)
                em.wait("scalar", sTE, mmv)
                em.wait("scalar", sOUT, vspill)
                ve = em.inc("scalar", lambda e: e.activation(out=ptm[:, :], in_=pQ[:, :], func=AF.Copy), sAC)
                em.wait("scalar", sAC, ve)
                vspill = dma_out(
                    bass.AP(VS, (4 * c + tt) * 128 * 1024, [[1024, 128], [1, 1024]]), ptm[:, :])

                em.wait("tensor", sAC, ve)
                mmq = proj_mms(p1wq, tt)
                em.wait("scalar", sTE, mmq)
                em.wait("scalar", sOUT, vspill)
                qe = em.inc("scalar", lambda e: e.activation(out=ptm[:, :], in_=pQ[:, :], func=AF.Copy), sAC)
                em.wait("vector", sAC, qe)
                em.wait("vector", sIN, vtq)
                xq_done = xpos_emit(tq[0], tq[1])
                em.wait("sync", sVE, xq_done)
                dma_in(bass.AP(TCK, row0 * 512, [[512, 128], [1, 512]]), tq[0][:, :])
                vtk = dma_in(bass.AP(TSK, row0 * 512, [[512, 128], [1, 512]]), tq[1][:, :])
                em.wait("tensor", sVE, xq_done)
                em.wait("tensor", sVE, qk_evac)
                tpq = tpose8(pxx)
                em.wait("vector", sTE, tpq)
                if tt == 0:
                    em.wait("vector", sOUT, spillq)
                qk_evac = em.inc(
                    "vector",
                    lambda e, tt=tt: e.tensor_copy(qfmb3[:, :, 128 * tt:128 * tt + 128], pT3),
                    sVE)

                em.wait("tensor", sAC, qe)
                mmk = proj_mms(p1wk, tt)
                em.wait("scalar", sTE, mmk)
                em.wait("scalar", sVE, xq_done)
                ke = em.inc("scalar", lambda e: e.activation(out=ptm[:, :], in_=pQ[:, :], func=AF.Copy), sAC)
                em.wait("vector", sAC, ke)
                em.wait("vector", sIN, vtk)
                xk_done = xpos_emit(tq[0], tq[1])
                em.wait("tensor", sVE, xk_done)
                em.wait("tensor", sVE, qk_evac)
                tpk = tpose8(pxx)
                tab_free[par] = tpk
                em.wait("vector", sTE, tpk)
                qk_evac = em.inc(
                    "vector",
                    lambda e, c=c, tt=tt: e.tensor_copy(
                        kfm3[:, :, c * 512 + 128 * tt: c * 512 + 128 * tt + 128], pT3),
                    sVE)
                last_mm = tpk

            em.wait("scalar", sVE, qk_evac)
            spillq = dma_out(
                bass.AP(QF, c * 512, [[L, 128], [128 * L, 8], [1, 512]]), qfmb3)

        def emit_quant(w, out_bf, src_ready=None):
            nb = w // 16
            if src_ready is not None:
                em.wait("scalar", src_ready[0], src_ready[1])
            vf = em.inc("vector", lambda e, w=w, nb=nb: e.tensor_reduce(
                qbm[:, :nb], qsrc.rearrange("p (nb k) -> p nb k", k=16)[:, :nb, :],
                AXX, ALU.max, apply_absolute_value=True), sVE)
            em.wait("vector", sVE, vf)
            vf = em.inc("vector", TS(qsc[:, :nb], qbm[:, :nb], 1.0 / 6.0, 1e-12, ALU.mult, ALU.max), sVE)
            em.wait("vector", sVE, vf)
            em.on("vector", TS(qs4[:, :nb], qsc[:, :nb], 0.25, None, ALU.mult))
            vf = em.inc("vector", lambda e, nb=nb: e.reciprocal(qrc[:, :nb], qsc[:, :nb]), sVE)
            em.wait("vector", sVE, vf)
            vsc = em.inc("vector", TT(qrc2[:, :nb], qrc[:, :nb], qrc[:, :nb], ALU.mult), sVE)
            em.wait("vector", sVE, vsc)
            s4b = bass.AP(qs4.tensor, qs4.offset, [[ARF_N, 128], [1, nb], [0, 16]])
            sq = em.inc("scalar", lambda e, w=w: e.activation(
                out=qu[:, :w], in_=qsrc[:, :w], func=AF.Square), sAC)
            rcb = bass.AP(qrc2.tensor, qrc2.offset, [[ARF_N, 128], [1, nb], [0, 16]])
            em.wait("vector", sAC, sq)
            vu = em.inc("vector", TT(qu2[:, :w], qu[:, :w], rcb, ALU.mult), sVE)
            em.wait("scalar", sVE, vu)
            sg = em.inc("scalar", lambda e, w=w: e.activation(
                out=qbSG[:, :w], in_=qsrc[:, :w], func=AF.Sign), sAC)

            def cmp(dst, k):
                return em.inc("scalar", lambda e, dst=dst, k=k, w=w: e.activation(
                    out=dst[:, :w], in_=qu2[:, :w], func=AF.Sign,
                    bias=cstb[:, 62 + k:63 + k]), sAC)

            cmp(qbA, 0)
            a2 = cmp(qbB, 1)
            em.wait("vector", sAC, a2)
            v1 = em.inc("vector", TT(qbS1[:, :w], qbA[:, :w], qbB[:, :w], ALU.add), sVE)
            em.wait("scalar", sVE, v1)
            cmp(qbA, 2)
            a4 = cmp(qbB, 3)
            em.wait("vector", sAC, a4)
            v2 = em.inc("vector", TT(qbS2[:, :w], qbA[:, :w], qbB[:, :w], ALU.add), sVE)
            em.wait("scalar", sVE, v2)
            cmp(qbA, 4)
            a6 = cmp(qbB, 5)
            em.wait("vector", sAC, a6)
            v3 = em.inc("vector", TT(qbS3[:, :w], qbA[:, :w], qbB[:, :w], ALU.add), sVE)
            em.wait("scalar", sVE, v3)
            a7 = cmp(qbA, 6)
            em.on("vector", TT(qbB[:, :w], qbS1[:, :w], qbS2[:, :w], ALU.add))
            em.wait("vector", sAC, a7)
            em.on("vector", lambda e, w=w: e.scalar_tensor_tensor(
                qbS2[:, :w], qbS3[:, :w], 2.0, qbB[:, :w], ALU.mult, ALU.add))
            em.on("vector", lambda e, w=w: e.scalar_tensor_tensor(
                qbS1[:, :w], qbA[:, :w], 4.0, qbS2[:, :w], ALU.mult, ALU.add))
            em.wait("vector", sAC, sg)
            vf1 = em.inc("vector", lambda e, w=w: e.scalar_tensor_tensor(
                qbS3[:, :w], qbS1[:, :w], 12.0, qbSG[:, :w], ALU.add, ALU.mult), sVE)
            em.wait("gpsimd", sVE, vf1)
            g1 = em.inc("gpsimd", TT(out_bf[:, :w], qbS3[:, :w], s4b, ALU.mult), sGP)
            em.wait("vector", sGP, g1)
            return em.inc("vector", lambda e: e.tensor_copy(qrc[:, 0:1], qbm[:, 0:1]), sVE)

        def thb(k, w):
            col = cstb[:, 71 + k:72 + k]
            return bass.AP(col.tensor, col.offset, [[80, 128], [0, w]])

        def emit_quant_v(w, out_bf, bf, st, src_ready=None):
            nb = w // 16
            A, Bb, S1, S2, S3, SG = bf["A"], bf["B"], bf["S1"], bf["S2"], bf["S3"], bf["SG"]
            qup, qu2p = bf["qu"], bf["qu2"]
            bm, sc, s2, rc, rc2 = bf["bm"], bf["sc"], bf["s2"], bf["rc"], bf["rc2"]
            em.wait("vector", sGP, st["g1"])
            vf = em.inc("vector", lambda e, nb=nb: e.tensor_reduce(
                bm[:, :nb], qsrc.rearrange("p (nb k) -> p nb k", k=16)[:, :nb, :],
                AXX, ALU.max, apply_absolute_value=True), sVE)
            em.wait("vector", sVE, vf)
            vf = em.inc("vector", TS(sc[:, :nb], bm[:, :nb], 1.0 / 6.0, 1e-12, ALU.mult, ALU.max), sVE)
            em.wait("vector", sVE, vf)
            em.on("vector", TS(s2[:, :nb], sc[:, :nb], 0.5, None, ALU.mult))
            vf = em.inc("vector", lambda e, nb=nb: e.reciprocal(rc[:, :nb], sc[:, :nb]), sVE)
            em.wait("vector", sVE, vf)
            vf = em.inc("vector", TT(rc2[:, :nb], rc[:, :nb], rc[:, :nb], ALU.mult), sVE)
            em.wait("vector", sVE, vf)
            em.on("vector", TT(qup[:, :w], qsrc[:, :w], qsrc[:, :w], ALU.mult))
            rcb = bass.AP(rc2.tensor, rc2.offset, [[ARF_N, 128], [1, nb], [0, 16]])
            qv = em.inc("vector", TT(qu2p[:, :w], qup[:, :w], rcb, ALU.mult), sVE)
            if src_ready is not None:
                em.wait("scalar", src_ready[0], src_ready[1])
            em.wait("scalar", sVE, st["f1"])
            sgn = em.inc("scalar", lambda e, w=w: e.activation(
                out=SG[:, :w], in_=qsrc[:, :w], func=AF.Sign), sAC)
            em.on("vector", TT(A[:, :w], qu2p[:, :w], thb(0, w), ALU.is_gt))
            em.on("vector", TT(Bb[:, :w], qu2p[:, :w], thb(1, w), ALU.is_gt))
            em.on("vector", TT(S1[:, :w], A[:, :w], Bb[:, :w], ALU.add))
            em.on("vector", TT(A[:, :w], qu2p[:, :w], thb(2, w), ALU.is_gt))
            em.on("vector", TT(Bb[:, :w], qu2p[:, :w], thb(3, w), ALU.is_gt))
            em.on("vector", TT(S2[:, :w], A[:, :w], Bb[:, :w], ALU.add))
            em.on("vector", TT(A[:, :w], qu2p[:, :w], thb(4, w), ALU.is_gt))
            em.on("vector", TT(Bb[:, :w], qu2p[:, :w], thb(5, w), ALU.is_gt))
            em.on("vector", TT(S3[:, :w], A[:, :w], Bb[:, :w], ALU.add))
            em.on("vector", TT(A[:, :w], qu2p[:, :w], thb(6, w), ALU.is_gt))
            em.on("vector", TT(Bb[:, :w], S1[:, :w], S2[:, :w], ALU.add))
            em.on("vector", lambda e, w=w, S3=S3, S1=S1, Bb=Bb: e.scalar_tensor_tensor(
                S1[:, :w], S3[:, :w], 2.0, Bb[:, :w], ALU.mult, ALU.add))
            em.on("vector", lambda e, w=w, A=A, S2=S2, S1=S1: e.scalar_tensor_tensor(
                S2[:, :w], A[:, :w], 4.0, S1[:, :w], ALU.mult, ALU.add))
            em.wait("vector", sAC, sgn)
            f1 = em.inc("vector", TT(S3[:, :w], S2[:, :w], SG[:, :w], ALU.mult), sVE)
            st["f1"] = f1
            s2b = bass.AP(s2.tensor, s2.offset, [[ARF_N, 128], [1, nb], [0, 16]])
            em.wait("gpsimd", sVE, f1)
            g1 = em.inc("gpsimd", TT(out_bf[:, :w], S3[:, :w], s2b, ALU.mult), sGP)
            st["g1"] = g1
            return {"g1": g1, "v": qv, "a": sgn}

        bufs_p2v = {"A": qbA, "B": qbB, "S1": qbS1, "S2": qbS2, "S3": qbS3,
                    "SG": qbSG, "qu": qu, "qu2": qu2,
                    "bm": vbm, "sc": vsc, "s2": vs2, "rc": vrc, "rc2": vrc2}
        bufs_p3v = {"A": vA, "B": vB, "S1": vS1, "S2": vS2, "S3": vS3,
                    "SG": vSG, "qu": pVQ, "qu2": pVU,
                    "bm": vbm, "sc": vsc, "s2": vs2, "rc": vrc, "rc2": vrc2}
        st_p2v = {"f1": 0, "g1": 0}
        st_p3v = {"f1": 0, "g1": 0}

        wq_tiles = []
        for Wsrc, tag, rows, cols in ((WG, "g", 22, 1024), (WU, "u", 22, 1024), (WD, "d", 8, 2816)):
            ccs = [0, 512] if cols == 1024 else [0, 512, 1024, 1536, 2048, 2560]
            for rt in range(rows):
                for c0 in ccs:
                    wq_tiles.append((Wsrc, tag, rt, c0, min(512, cols - c0), cols))

        wq_state = {"busy_v": 0, "busy_a": 0, "busy_o": 0, "busy_t": 0}

        def emit_wq_tile(ti):
            Wsrc, tag, rt, c0, w, cols = wq_tiles[ti]
            em.wait("sync", sVE, wq_state["busy_v"])
            em.wait("sync", sAC, wq_state["busy_a"])
            vw = dma_in(bass.AP(Wsrc, rt * 128 * cols + c0, [[cols, 128], [1, w]]), qsrc[:, :w])
            em.wait("vector", sIN, vw)
            em.wait("vector", sOUT, wq_state["busy_o"])
            ret = emit_quant_v(w, wqo, bufs_p2v, st_p2v, src_ready=(sIN, vw))
            wq_state["busy_v"] = ret["v"]
            wq_state["busy_a"] = ret["a"]
            ntp = w // 128
            em.wait("tensor", sGP, ret["g1"])
            em.wait("tensor", sVE, wq_state["busy_t"])
            r = 0
            for k in range(ntp):
                fn = lambda e, k=k: e.transpose(pT4[:, 128 * k:128 * k + 128], wqo[:, 128 * k:128 * k + 128], idb[:, :])
                if k == ntp - 1:
                    r = em.inc("tensor", fn, sTE)
                else:
                    em.on("tensor", fn)
            em.wait("vector", sTE, r)
            ev = em.inc("vector", lambda e, ntp=ntp: e.tensor_copy(qbA[:, :128 * ntp], pT4[:, :128 * ntp]), sVE)
            wq_state["busy_t"] = ev
            em.wait("scalar", sVE, ev)
            o = 0
            for k in range(ntp):
                blk = (c0 + 128 * k) // 128
                if tag == "g":
                    dst = bass.AP(WGT, blk * 128 * II + rt * 128, [[II, 128], [1, 128]])
                elif tag == "u":
                    dst = bass.AP(WUT, blk * 128 * II + rt * 128, [[II, 128], [1, 128]])
                else:
                    dst = bass.AP(WDT, blk * 128 * H + rt * 128, [[H, 128], [1, 128]])
                o = em.inc("scalar", lambda e, dst=dst, k=k: e.dma_start(
                    out=dst, in_=qbA[:, 128 * k:128 * k + 128]), sOUT, 16)
            wq_state["busy_o"] = o

        wq_cursor = [0]

        def emit_wq_batch(n):
            while n > 0 and wq_cursor[0] < len(wq_tiles):
                emit_wq_tile(wq_cursor[0])
                wq_cursor[0] += 1
                n -= 1

        p1_T = em.val(sTE)
        p1_V = em.val(sVE)
        p1_O = em.val(sOUT)
        em.wait("sync", sTE, p1_T)
        em.wait("sync", sVE, p1_V)
        em.wait("sync", sOUT, p1_O)
        dma_in(bass.AP(VS, 0, [[1024, 128], [128 * 1024, 32], [1, 1024]]),
               vtm.rearrange("p (t f) -> p t f", f=1024))

        ssb3 = ssb.rearrange("p (t f) -> p t f", f=QB)

        def uslot(kt):
            return uextra[:, :] if kt == 0 else ssb3[:, kt - 1, :]
        vtm3 = vtm.rearrange("p (t f) -> p t f", f=1024)
        qblk3 = qblk.rearrange("p (hc f) -> p hc f", f=QB)

        pv_ev = 0
        prev_ze = 0
        prev_evz = 0
        prev_ssb_free = 0
        prev_dg = 0
        last_smm = 0
        for qb in range(NQB):
            em.wait("sync", sTE, last_smm)
            vq = dma_in(bass.AP(QF, qb * QB, [[L, 128], [128 * L, 8], [1, QB]]), qblk3)

            em.wait("tensor", sIN, vq)
            em.wait("tensor", sAC, prev_ze)
            evs = {}
            for kt in range(NKT):
                par = kt % 2
                if kt >= 2:
                    em.wait("tensor", sAC, evs[kt - 2])
                for hc in range(8):
                    fn = lambda e, kt=kt, hc=hc, par=par: e.matmul(
                        pS[par],
                        kfm[:, 4096 * hc + 128 * kt: 4096 * hc + 128 * kt + 128],
                        qblk3[:, hc, :],
                        start=(hc == 0), stop=(hc == 7))
                    if hc == 7:
                        mms = em.inc("tensor", fn, sTE)
                    else:
                        em.on("tensor", fn)
                em.wait("scalar", sTE, mms)
                if kt == 0:
                    em.wait("scalar", sTE, prev_ssb_free)
                evs[kt] = em.inc("scalar", lambda e, kt=kt, par=par: e.activation(
                    out=ssb3[:, kt, :], in_=pS[par], func=AF.Copy), sAC)
            last_smm = mms

            em.wait("vector", sAC, evs[NKT - 1])
            vmx = em.inc("vector", lambda e: e.tensor_reduce(
                mxf[:, :], bass.AP(ssb.tensor, ssb.offset, [[ARB_N, 128], [1, QB], [QB, NKT]]),
                AXX, ALU.max), sVE)
            em.wait("tensor", sVE, vmx)
            em.wait("tensor", sAC, prev_dg)
            for c2 in range(2):
                fn = lambda e, c2=c2: e.transpose(
                    pA4[:, 128 * c2:128 * c2 + 128], mxf[:, 128 * c2:128 * c2 + 128], idf[:, :])
                if c2 == 1:
                    tmx = em.inc("tensor", fn, sTE)
                else:
                    em.on("tensor", fn)
            em.wait("vector", sTE, tmx)
            em.on("vector", lambda e: e.reduce_max(mq4[:, 0:1], pA4[:, 0:128], AXX))
            vf = em.inc("vector", lambda e: e.reduce_max(mq4[:, 1:2], pA4[:, 128:256], AXX), sVE)
            em.wait("vector", sVE, vf)
            em.on("vector", lambda e: e.tensor_copy(mq4b[:, 0:1], mq4[:, 0:1]))
            vmq = em.inc("vector", lambda e: e.tensor_copy(mq4b[:, 1:2], mq4[:, 1:2]), sVE)
            em.wait("tensor", sVE, vmq)
            tmr1 = em.inc("tensor", lambda e: e.transpose(
                pMR2[:1, 0:128], mq4b[:, 0:1], idb[:, :]), sTE)
            em.wait("scalar", sTE, tmr1)
            ev1 = em.inc("scalar", lambda e: e.activation(
                out=mrowB[:1, 0:128], in_=pMR2[:1, 0:128], func=AF.Copy), sAC)
            em.wait("tensor", sAC, ev1)
            tmr2 = em.inc("tensor", lambda e: e.transpose(
                pMR2[:1, 0:128], mq4b[:, 1:2], idb[:, :]), sTE)
            em.wait("scalar", sTE, tmr2)
            evmr = em.inc("scalar", lambda e: e.activation(
                out=mrowB[:1, 128:256], in_=pMR2[:1, 0:128], func=AF.Copy), sAC)
            if DEBUG_SCRATCH and qb == 0:
                em.wait("scalar", sAC, evmr)
                dma_out(DM2.ap(), mq4)
                dma_out(DM3.ap(), mrowB[:1, :])
            em.wait("tensor", sAC, evmr)
            tmb = em.inc("tensor", lambda e: e.matmul(
                pA4, onesb[:1, :], mrowB[:1, :], start=True, stop=True), sTE)
            em.wait("scalar", sTE, tmb)
            evmb = em.inc("scalar", lambda e: e.activation(out=mbc[:, :], in_=pA4, func=AF.Copy), sAC)
            if DEBUG_SCRATCH and qb == 0:
                em.wait("scalar", sAC, evmb)
                dma_out(DS.ap(), ssb)
                dma_out(DM.ap(), mbc)
                dma_out(DM4.ap(), mxf)

            zmm = {}
            ud = 0
            for kt in range(NKT):
                par = kt % 2
                if kt < 2:
                    em.wait("vector", sAC, evmb)
                else:
                    em.wait("vector", sAC, evs_exp[kt - 2])
                vsub = em.inc("vector", lambda e, kt=kt, par=par: e.tensor_tensor(
                    pS[par], ssb3[:, kt, :], mbc[:, :], ALU.subtract), sVE)
                em.wait("scalar", sVE, vsub)
                if kt == 0:
                    evs_exp = {}
                evs_exp[kt] = em.inc("scalar", lambda e, kt=kt, par=par: e.activation(
                    out=ssb3[:, kt, :], in_=pS[par], func=AF.Exp), sAC)
                oidx = kt - 2 * qb + 30
                if kt >= 2 or qb > 0:
                    em.wait("scalar", sVE, ud)
                em.on("scalar", lambda e, par=par, oidx=oidx: e.activation(
                    out=pA4, in_=bio[:, 0:QB], func=AF.Abs, bias=cstb[:, oidx:oidx + 1]))
                dg = em.inc("scalar", lambda e, par=par: e.activation(
                    out=dtb[par][:, :], in_=pA4, func=AF.Exp, scale=cstb[:, 69:70]), sAC)
                em.wait("tensor", sAC, evs_exp[kt])
                if kt == 0:
                    em.wait("tensor", sAC, prev_evz)
                zmm[kt] = em.inc("tensor", lambda e, kt=kt: e.matmul(
                    pZ, onesc[:, :1], ssb3[:, kt, :],
                    start=(kt == 0), stop=(kt == NKT - 1)), sTE)
                em.wait("vector", sAC, dg)
                em.wait("vector", sTE, zmm[kt])
                ud = em.inc("vector", lambda e, kt=kt, par=par: e.tensor_tensor(
                    uslot(kt), ssb3[:, kt, :], dtb[par][:, :], ALU.mult), sVE)
            prev_ze = evs_exp[NKT - 1]
            prev_dg = dg

            em.wait("scalar", sTE, zmm[NKT - 1])
            evz = em.inc("scalar", lambda e: e.activation(out=zrow[:1, :], in_=pZ, func=AF.Copy), sAC)
            prev_evz = evz
            em.wait("vector", sAC, evz)
            vf = em.inc("vector", lambda e: e.reciprocal(rzf[:1, :], zrow[:1, :]), sVE)
            em.wait("vector", sVE, vf)
            vrz = em.inc("vector", lambda e: e.tensor_copy(rzb[:1, :], rzf[:1, :]), sVE)
            em.wait("tensor", sVE, vrz)
            trz1 = em.inc("tensor", lambda e: e.transpose(
                pRZB[:, 0:1], rzb[:1, 0:128], onesb[:1, :1]), sTE)
            em.wait("scalar", sTE, trz1)
            evr1 = em.inc("scalar", lambda e: e.activation(
                out=rzc[:, 0:1], in_=pRZB[:, 0:1], func=AF.Copy), sAC)
            em.wait("tensor", sAC, evr1)
            trz2 = em.inc("tensor", lambda e: e.transpose(
                pRZB[:, 0:1], rzb[:1, 128:256], onesb[:1, :1]), sTE)
            em.wait("scalar", sTE, trz2)
            evrz = em.inc("scalar", lambda e: e.activation(
                out=rzc[:, 1:2], in_=pRZB[:, 0:1], func=AF.Copy), sAC)

            if DEBUG_SCRATCH and qb == 0:
                em.wait("scalar", sVE, ud)
                em.wait("scalar", sAC, evrz)
                dma_out(DM5.ap(), ssb)
                dma_out(DM6.ap(), uextra)
                dma_out(DM7.ap(), zrow[:1, :])
                dma_out(DM8.ap(), rzc)
            em.wait("tensor", sVE, ud)
            for qs in range(2):
                em.wait("sync", sVE, pv_ev)
                em.wait("sync", sOUT, em.val(sOUT))
                vxr = dma_in(bass.AP(X, (qb * QB + qs * 128) * 1024, [[1024, 128], [1, 1024]]), xres[:, :])
                em.wait("tensor", sVE, pv_ev)
                for kt in range(NKT):
                    for nv in range(2):
                        fn = lambda e, kt=kt, qs=qs, nv=nv: e.matmul(
                            pPV[:, 512 * nv:512 * nv + 512],
                            uslot(kt)[:, 128 * qs:128 * qs + 128],
                            vtm3[:, kt, 512 * nv:512 * nv + 512],
                            start=(kt == 0), stop=(kt == NKT - 1))
                        if kt == NKT - 1 and nv == 1:
                            mpv = em.inc("tensor", fn, sTE)
                        else:
                            em.on("tensor", fn)
                em.wait("vector", sTE, mpv)
                em.wait("vector", sAC, evrz)
                em.wait("vector", sIN, vxr)
                em.on("vector", lambda e, qs=qs: e.scalar_tensor_tensor(
                    ypre[:, :], pPV[:, :], rzc[:, qs:qs + 1], xres[:, :], ALU.mult, ALU.add))
                vf = em.inc("vector", lambda e: e.reduce_sum(stat[:, 0:1], ypre[:, :], AXX), sVE)
                em.wait("vector", sVE, vf)
                vf = em.inc("vector", TS(stat[:, 1:2], stat[:, 0:1], 1.0 / 1024.0, None, ALU.mult), sVE)
                em.wait("vector", sVE, vf)
                em.on("vector", lambda e: e.tensor_scalar(ypc[:, :], ypre[:, :], stat[:, 1:2], None, ALU.subtract))
                vvar = em.inc("vector", lambda e: e.scalar_tensor_tensor(
                    xres[:, :], ypc[:, :], 1.0, ypc[:, :], ALU.bypass, ALU.mult,
                    accum_out=stat[:, 2:3]), sVE)
                pv_ev = vvar
                em.wait("vector", sVE, vvar)
                vpe = em.inc("vector", TS(stat[:, 4:5], stat[:, 2:3], 1.0 / 1024.0, EPS, ALU.mult, ALU.add), sVE)
                em.wait("scalar", sVE, vpe)
                em.on("scalar", lambda e: e.activation(out=stat[:, 5:6], in_=stat[:, 4:5], func=AF.Ln))
                rstd = em.inc("scalar", lambda e: e.activation(
                    out=stat[:, 6:7], in_=stat[:, 5:6], func=AF.Exp, scale=cstb[:, 70:71]), sAC)
                em.wait("vector", sAC, rstd)
                yv = em.inc("vector", lambda e: e.tensor_scalar(
                    xres[:, :], ypc[:, :], stat[:, 6:7], None, ALU.mult), sVE)
                em.wait("scalar", sVE, yv)
                if DEBUG_SCRATCH and qb == 0 and qs == 0:
                    dma_out(DM9.ap(), ypre)
                    dma_out(DM10.ap(), stat)
                dma_out(
                    bass.AP(YS, (qb * QB + qs * 128) * 1024, [[1024, 128], [1, 1024]]), xres[:, :])
            prev_ssb_free = mpv
            emit_wq_batch(9)
        emit_wq_batch(len(wq_tiles))

        p2_T = em.val(sTE)
        p2_V = em.val(sVE)
        p2_O = em.val(sOUT)
        em.wait("sync", sTE, p2_T)
        em.wait("sync", sVE, p2_V)
        em.wait("sync", sOUT, p2_O)
        em.wait("vector", sGP, em.val(sGP))
        dma_in(bass.AP(WGT, 0, [[II, 128], [128 * II, 8], [1, II]]),
               wgTr.rearrange("p (hc f) -> p hc f", f=II))
        dma_in(bass.AP(WUT, 0, [[II, 128], [128 * II, 8], [1, II]]),
               wuTr.rearrange("p (hc f) -> p hc f", f=II))
        vwl = dma_in(bass.AP(WDT, 0, [[H, 128], [128 * H, 22], [1, H]]),
                     wdTr.rearrange("p (ic f) -> p ic f", f=H))

        wgTr3 = wgTr.rearrange("p (hc f) -> p hc f", f=II)
        wuTr3 = wuTr.rearrange("p (hc f) -> p hc f", f=II)
        wdTr3 = wdTr.rearrange("p (ic f) -> p ic f", f=H)
        ICH = [(0, 512), (512, 512), (1024, 512), (1536, 512), (2048, 512), (2560, 256)]

        p3qc = [0]

        def p3_quant(w, out_bf, src_ready):
            i = p3qc[0]
            p3qc[0] += 1
            if i % 2 == 0:
                ret = emit_quant_v(w, out_bf, bufs_p3v, st_p3v, src_ready=src_ready)
                return (sGP, ret["g1"])
            vq = emit_quant(w, out_bf, src_ready=src_ready)
            return (sVE, vq)

        yload = 0
        out_done = 0
        prev_yq_T = 0
        prev_hq_T = 0
        pT_ev = 0
        yqT_ev = 0
        hms_prev = {}
        yti_free = 0
        for t in range(32):
            em.wait("sync", sVE, yti_free)
            vy = dma_in(bass.AP(YS, t * 128 * 1024, [[1024, 128], [1, 1024]]), yti[:, :])
            em.wait("vector", sIN, vy)
            yreadies = []
            for yc in range(2):
                em.wait("vector", sTE, prev_yq_T)
                em.wait("gpsimd", sTE, prev_yq_T)
                ycp = em.inc("vector", lambda e, yc=yc: e.tensor_copy(
                    qsrc[:, :], yti[:, 512 * yc:512 * yc + 512]), sVE)
                yreadies.append(p3_quant(512, yqb[:, 512 * yc:512 * yc + 512], (sVE, ycp)))
            for sm, vv_ in yreadies:
                em.wait("tensor", sm, vv_)
            em.wait("tensor", sVE, pT_ev)
            if t == 0:
                em.wait("tensor", sIN, vwl)
            r = 0
            for hc in range(8):
                fn = lambda e, hc=hc: e.transpose(
                    pT[:, 128 * hc:128 * hc + 128], yqb[:, 128 * hc:128 * hc + 128], idb[:, :])
                if hc == 7:
                    r = em.inc("tensor", fn, sTE)
                else:
                    em.on("tensor", fn)
            prev_yq_T = r
            em.wait("vector", sTE, r)
            yqT_ev = em.inc("vector", lambda e: e.tensor_copy(yqT[:, :], pT[:, :]), sVE)
            pT_ev = yqT_ev

            for ci, (c0, w) in enumerate(ICH):
                par = ci % 2
                ntp = w // 128
                if ci >= 2:
                    em.wait("tensor", sVE, hms_prev[ci - 2])
                em.wait("tensor", sVE, yqT_ev)
                for hc in range(8):
                    fn = lambda e, hc=hc, c0=c0, w=w, par=par: e.matmul(
                        pG[par][:, 0:w], yqT[:, 128 * hc:128 * hc + 128],
                        wgTr3[:, hc, c0:c0 + w], start=(hc == 0), stop=(hc == 7))
                    if hc == 7:
                        mg = em.inc("tensor", fn, sTE)
                    else:
                        em.on("tensor", fn)
                em.wait("scalar", sTE, mg)
                em.wait("scalar", sVE, hms_prev.get(("tile", ci), 0))
                gev = em.inc("scalar", lambda e, c0=c0, w=w, par=par: e.activation(
                    out=gs[:, c0:c0 + w], in_=pG[par][:, 0:w], func=AF.Silu), sAC)
                em.wait("tensor", sAC, gev)
                for hc in range(8):
                    fn = lambda e, hc=hc, c0=c0, w=w, par=par: e.matmul(
                        pG[par][:, 0:w], yqT[:, 128 * hc:128 * hc + 128],
                        wuTr3[:, hc, c0:c0 + w], start=(hc == 0), stop=(hc == 7))
                    if hc == 7:
                        mu = em.inc("tensor", fn, sTE)
                    else:
                        em.on("tensor", fn)
                em.wait("vector", sTE, mu)
                hm = em.inc("vector", lambda e, c0=c0, w=w, par=par: e.tensor_tensor(
                    qsrc[:, :w], pG[par][:, 0:w], gs[:, c0:c0 + w], ALU.mult), sVE)
                hms_prev[ci] = hm
                hms_prev[("tile", ci)] = hm
                em.wait("vector", sTE, prev_hq_T)
                em.wait("gpsimd", sTE, prev_hq_T)
                hready = p3_quant(w, wqo, (sVE, hm))
                em.wait("tensor", hready[0], hready[1])
                em.wait("tensor", sVE, pT_ev)
                r2 = 0
                for k in range(ntp):
                    fn = lambda e, k=k: e.transpose(
                        pT[:, 128 * k:128 * k + 128], wqo[:, 128 * k:128 * k + 128], idb[:, :])
                    if k == ntp - 1:
                        r2 = em.inc("tensor", fn, sTE)
                    else:
                        em.on("tensor", fn)
                prev_hq_T = r2
                em.wait("vector", sTE, r2)
                pT_ev = em.inc("vector", lambda e, c0=c0, ntp=ntp: e.tensor_copy(
                    hqT[:, c0:c0 + 128 * ntp], pT[:, :128 * ntp]), sVE)

            em.wait("tensor", sVE, pT_ev)
            em.wait("tensor", sVE, yload)
            for nd in range(2):
                for ic in range(22):
                    fn = lambda e, ic=ic, nd=nd: e.matmul(
                        pPV[:, 512 * nd:512 * nd + 512], hqT[:, 128 * ic:128 * ic + 128],
                        wdTr3[:, ic, 512 * nd:512 * nd + 512],
                        start=(ic == 0), stop=(ic == 21))
                    if ic == 21 and nd == 1:
                        md = em.inc("tensor", fn, sTE)
                    else:
                        em.on("tensor", fn)
            em.wait("vector", sTE, md)
            em.wait("vector", sOUT, out_done)
            yload = em.inc("vector", lambda e: e.scalar_tensor_tensor(
                ypre2[:, :], pPV[:, :], 1.0, yti[:, :], ALU.bypass, ALU.add), sVE)
            vf = em.inc("vector", lambda e: e.reduce_sum(st3[:, 0:1], ypre2[:, :], AXX), sVE)
            em.wait("vector", sVE, vf)
            vf = em.inc("vector", TS(st3[:, 1:2], st3[:, 0:1], 1.0 / 1024.0, None, ALU.mult), sVE)
            em.wait("vector", sVE, vf)
            em.on("vector", lambda e: e.tensor_scalar(yti[:, :], ypre2[:, :], st3[:, 1:2], None, ALU.subtract))
            vv = em.inc("vector", lambda e: e.scalar_tensor_tensor(
                ypre2[:, :], yti[:, :], 1.0, yti[:, :], ALU.bypass, ALU.mult,
                accum_out=st3[:, 2:3]), sVE)
            em.wait("vector", sVE, vv)
            vpe3 = em.inc("vector", TS(st3[:, 4:5], st3[:, 2:3], 1.0 / 1024.0, EPS, ALU.mult, ALU.add), sVE)
            em.wait("scalar", sVE, vpe3)
            em.on("scalar", lambda e: e.activation(out=st3[:, 5:6], in_=st3[:, 4:5], func=AF.Ln))
            rstd3 = em.inc("scalar", lambda e: e.activation(
                out=st3[:, 6:7], in_=st3[:, 5:6], func=AF.Exp, scale=cstb[:, 70:71]), sAC)
            em.wait("vector", sAC, rstd3)
            if DEBUG_SCRATCH and t == 2:
                em.wait("sync", sAC, rstd3)
                dma_out(DM11.ap(), yti)
                dma_out(DM12.ap(), st3[:, 0:16])
            yout = em.inc("vector", lambda e: e.tensor_scalar(
                ypre2[:, :], yti[:, :], st3[:, 6:7], None, ALU.mult), sVE)
            yti_free = yout
            em.wait("scalar", sVE, yout)
            out_done = dma_out(
                bass.AP(OUT, t * 128 * 1024, [[1024, 128], [1, 1024]]), ypre2[:, :])

        em.wait("gpsimd", sOUT, em.val(sOUT))
        em.on("gpsimd", lambda e: e.nop())

        block = ctx.enter_context(nc.Block())

        @block.sync
        def _(e):
            for fn in em.q["sync"]:
                fn(e)

        @block.tensor
        def _(e):
            for fn in em.q["tensor"]:
                fn(e)

        @block.vector
        def _(e):
            for fn in em.q["vector"]:
                fn(e)

        @block.scalar
        def _(e):
            for fn in em.q["scalar"]:
                fn(e)

        @block.gpsimd
        def _(e):
            for fn in em.q["gpsimd"]:
                fn(e)

    return nc


def _mk_cst():
    cst = np.zeros((128, 80), dtype=np.float32)
    for j in range(62):
        cst[:, j] = 128.0 * (j - 30)
    for k in range(7):
        cst[:, 62 + k] = -TH2[k]
    cst[:, 69] = LNG
    cst[:, 70] = -0.5
    for k in range(7):
        cst[:, 71 + k] = TH2[k]
    return cst


def kernel(**inputs):
    Xall = np.ascontiguousarray(inputs["L0_attn_X"], dtype=np.float32)
    tcq, tsq, tck, tsk = _mk_tables()
    shared = {
        "wq_in": np.ascontiguousarray(inputs["L0_attn_W_Q"], dtype=np.float32),
        "wk_in": np.ascontiguousarray(inputs["L0_attn_W_K"], dtype=np.float32),
        "wv_in": np.ascontiguousarray(inputs["L0_attn_W_V"], dtype=np.float32),
        "wg_in": np.ascontiguousarray(inputs["L0_ffn_w_gate"], dtype=np.float32),
        "wu_in": np.ascontiguousarray(inputs["L0_ffn_w_up"], dtype=np.float32),
        "wd_in": np.ascontiguousarray(inputs["L0_ffn_w_down"], dtype=np.float32),
        "tcq_in": tcq, "tsq_in": tsq, "tck_in": tck, "tsk_in": tsk,
        "cst_in": _mk_cst(),
        "idb_in": np.eye(128, dtype=bfdt),
        "idf_in": np.eye(128, dtype=np.float32),
        "bio_in": (np.arange(128, dtype=np.float32)[:, None]
                   - np.arange(512, dtype=np.float32)[None, :]),
    }
    nc = build_nc()
    in_maps = [dict(shared, x_in=np.ascontiguousarray(Xall[c])) for c in range(NCORES)]
    res = run_bass_kernel_spmd(nc, in_maps, core_ids=list(range(NCORES)))
    return np.stack([res.results[c]["out"] for c in range(NCORES)], axis=0)


# version 08
# speedup vs baseline: 1.2046x; 1.2046x over previous
import math
from collections import defaultdict
from contextlib import ExitStack

import numpy as np
import ml_dtypes

import concourse.bass as bass
import concourse.mybir as mybir
from concourse.bass_utils import run_bass_kernel_spmd

F32 = mybir.dt.float32
BF16 = mybir.dt.bfloat16
AF = mybir.ActivationFunctionType
ALU = mybir.AluOpType
AXX = mybir.AxisListType.X
bfdt = ml_dtypes.bfloat16

B, L, H, II = 8, 4096, 1024, 2816
NCORES = 8
LNG = float(np.float32(math.log(0.9)))
EPS = 1e-5
NCH = 8
QB = 256
NQB = L // QB
NKT = L // 128
TH2 = [0.0625, 0.5625, 1.5625, 3.0625, 6.25, 12.25, 25.0]

ARB_N = 80896
ARF_N = 7168


def _mk_tables():
    i = np.arange(512, dtype=np.float64)
    scale_vec = (2.0 * i + 0.4 * H) / (1.4 * H)
    pos = np.arange(L, dtype=np.float64)
    scl = scale_vec[None, :] ** (pos / 512.0)[:, None]
    inv_freq = 1.0 / 10000.0 ** (i / 512.0)
    ang = pos[:, None] * inv_freq[None, :]
    c, s = np.cos(ang), np.sin(ang)
    return (
        (c * scl).astype(np.float32), (s * scl).astype(np.float32),
        (c / scl).astype(np.float32), (s / scl).astype(np.float32),
    )


class EM:
    ENGS = ("sync", "tensor", "vector", "scalar", "gpsimd")

    def __init__(self):
        self.q = {e: [] for e in self.ENGS}
        self.cnt = defaultdict(int)

    def on(self, eng, fn):
        self.q[eng].append(fn)

    def inc(self, eng, fn, sem, n=1):
        self.cnt[sem] += n
        self.q[eng].append(lambda e, fn=fn, sem=sem, n=n: fn(e).then_inc(sem, n))
        return self.cnt[sem]

    def wait(self, eng, sem, val):
        if val > 0:
            self.q[eng].append(lambda e, sem=sem, val=val: e.wait_ge(sem, val))

    def val(self, sem):
        return self.cnt[sem]


DEBUG_SCRATCH = False


def build_nc():
    nc = bass.Bass("TRN2", target_bir_lowering=False)
    _scr = "ExternalOutput" if DEBUG_SCRATCH else "Internal"

    X = nc.dram_tensor("x_in", [L, H], F32, kind="ExternalInput")
    WQ = nc.dram_tensor("wq_in", [H, H], F32, kind="ExternalInput")
    WK = nc.dram_tensor("wk_in", [H, H], F32, kind="ExternalInput")
    WV = nc.dram_tensor("wv_in", [H, H], F32, kind="ExternalInput")
    WG = nc.dram_tensor("wg_in", [II, H], F32, kind="ExternalInput")
    WU = nc.dram_tensor("wu_in", [II, H], F32, kind="ExternalInput")
    WD = nc.dram_tensor("wd_in", [H, II], F32, kind="ExternalInput")
    TCQ = nc.dram_tensor("tcq_in", [L, 512], F32, kind="ExternalInput")
    TSQ = nc.dram_tensor("tsq_in", [L, 512], F32, kind="ExternalInput")
    TCK = nc.dram_tensor("tck_in", [L, 512], F32, kind="ExternalInput")
    TSK = nc.dram_tensor("tsk_in", [L, 512], F32, kind="ExternalInput")
    IDB = nc.dram_tensor("idb_in", [128, 128], BF16, kind="ExternalInput")
    IDF = nc.dram_tensor("idf_in", [128, 128], F32, kind="ExternalInput")
    BIO = nc.dram_tensor("bio_in", [128, 512], F32, kind="ExternalInput")
    CST = nc.dram_tensor("cst_in", [128, 80], F32, kind="ExternalInput")
    OUT = nc.dram_tensor("out", [L, H], F32, kind="ExternalOutput")
    QF = nc.dram_tensor("qf_sc", [8, 128, L], BF16, kind=_scr)
    VS = nc.dram_tensor("vs_sc", [32, 128, H], BF16, kind=_scr)
    YS = nc.dram_tensor("ys_sc", [L, H], F32, kind=_scr)
    WGT = nc.dram_tensor("wgt_sc", [8, 128, II], BF16, kind=_scr)
    WUT = nc.dram_tensor("wut_sc", [8, 128, II], BF16, kind=_scr)
    WDT = nc.dram_tensor("wdt_sc", [22, 128, II // II * H], BF16, kind=_scr)
    DS = nc.dram_tensor("ds_sc", [128, 8192], BF16, kind=_scr) if DEBUG_SCRATCH else None
    DM = nc.dram_tensor("dm_sc", [128, 256], F32, kind=_scr) if DEBUG_SCRATCH else None
    DM2 = nc.dram_tensor("dm2_sc", [128, 2], F32, kind=_scr) if DEBUG_SCRATCH else None
    DM3 = nc.dram_tensor("dm3_sc", [1, 256], BF16, kind=_scr) if DEBUG_SCRATCH else None
    DM4 = nc.dram_tensor("dm4_sc", [128, 256], F32, kind=_scr) if DEBUG_SCRATCH else None
    DM5 = nc.dram_tensor("dm5_sc", [128, 8192], BF16, kind=_scr) if DEBUG_SCRATCH else None
    DM6 = nc.dram_tensor("dm6_sc", [128, 256], BF16, kind=_scr) if DEBUG_SCRATCH else None
    DM7 = nc.dram_tensor("dm7_sc", [1, 256], F32, kind=_scr) if DEBUG_SCRATCH else None
    DM8 = nc.dram_tensor("dm8_sc", [128, 2], F32, kind=_scr) if DEBUG_SCRATCH else None
    DM9 = nc.dram_tensor("dm9_sc", [128, 1024], F32, kind=_scr) if DEBUG_SCRATCH else None
    DM10 = nc.dram_tensor("dm10_sc", [128, 16], F32, kind=_scr) if DEBUG_SCRATCH else None
    DM11 = nc.dram_tensor("dm11_sc", [128, 1024], F32, kind=_scr) if DEBUG_SCRATCH else None
    DM12 = nc.dram_tensor("dm12_sc", [128, 16], F32, kind=_scr) if DEBUG_SCRATCH else None

    em = EM()

    with ExitStack() as ctx:
        def sb(name, shape, dt=F32):
            return ctx.enter_context(nc.sbuf_tensor(name, shape, dt))

        def sem(name):
            return ctx.enter_context(nc.semaphore(name))

        sIN = sem("sIN")
        sOUT = sem("sOUT")
        sTE = sem("sTE")
        sVE = sem("sVE")
        sAC = sem("sAC")
        sGP = sem("sGP")

        idb = sb("idb", [128, 128], BF16)
        idf = sb("idf", [128, 128], F32)
        bio = sb("bio", [128, 512], F32)
        onesb = sb("onesb", [1, 128], BF16)
        onesf = sb("onesf", [1, 128], F32)
        onesc = sb("onesc", [128, 1], BF16)
        cstb = sb("cstb", [128, 80], F32)
        ARB = sb("arb", [128, ARB_N], BF16)
        ARF = sb("arf", [128, ARF_N], F32)
        PF = ctx.enter_context(nc.psum_tensor("pallf", [128, 3584], F32))
        PB = ctx.enter_context(nc.psum_tensor("pallb", [128, 1024], BF16))

        def arb(o, n):
            assert o + n <= ARB_N
            return ARB[:, o:o + n]

        def arf(o, n):
            assert o + n <= ARF_N
            return ARF[:, o:o + n]

        kfm = arb(0, 32768)
        p1wq = arb(32768, 8192)
        p1wk = arb(40960, 8192)
        p1wv = arb(49152, 8192)
        xbf = arb(57344, 4096)
        xfm = arb(61440, 4096)
        qfmb = arb(65536, 4096)
        ptm = arb(69632, 1024)
        pxx = arb(70656, 1024)
        xin = arf(0, 4096)
        tt0 = [arf(4096, 512), arf(4608, 512)]
        tt1 = [arf(5120, 512), arf(5632, 512)]
        pxa = arf(6144, 512)
        pxb = arf(6656, 512)
        vtm = arb(32768, 32768)
        ssb = arb(65536, 8192)
        qblk = arb(73728, 2048)
        dtb = [arb(75776, 256), arb(76032, 256)]
        qbA = arb(76288, 512)
        qbB = arb(76800, 512)
        qbS1 = arb(77312, 512)
        qbS2 = arb(77824, 512)
        qbS3 = arb(78336, 512)
        qbSG = arb(78848, 512)
        wqo = arb(79360, 512)
        mq4b = arb(79872, 2)
        mrowB = arb(79876, 256)
        rzb = arb(80132, 256)
        uextra = arb(80388, 256)
        xres = arf(0, 1024)
        ypre = arf(1024, 1024)
        mxf = arf(2048, 256)
        mbc = arf(2304, 256)
        zrow = arf(2560, 256)
        rzf = arf(2816, 256)
        mq4 = arf(3072, 2)
        rzc = arf(3076, 2)
        stat = arf(3088, 16)
        ypc = arf(3104, 1024)
        qsrc = arf(5200, 512)
        qbm = arf(5712, 32)
        qsc = arf(5744, 32)
        qs4 = arf(5776, 32)
        qrc = arf(5808, 32)
        qrc2 = arf(5840, 32)
        qu = arf(5872, 512)
        qu2 = arf(6384, 512)
        wgTr = arb(0, 22528)
        wuTr = arb(22528, 22528)
        wdTr = arb(45056, 22528)
        yqb = arb(67584, 1024)
        yqT = arb(68608, 1024)
        hqT = arb(69632, 2816)
        qca = arb(72448, 1408)
        qcb = arb(73856, 1408)
        qs1 = arb(75264, 1408)
        qsg = arb(76672, 1408)
        qqo = arb(78080, 1408)
        qft = arb(79488, 1408)
        gs = arf(0, 2816)
        yti = arf(2816, 1024)
        ypre2 = arf(3840, 1024)
        st3 = arf(7136, 32)
        vbm = arf(6896, 32)
        vsc = arf(6928, 32)
        vs2 = arf(6960, 32)
        vrc = arf(6992, 32)
        vrc2 = arf(7024, 32)
        vA = arb(72448, 512)
        vB = arb(72960, 512)
        vS1 = arb(73472, 512)
        vS2 = arb(73984, 512)
        vS3 = arb(74496, 512)
        vSG = arb(75008, 512)
        pQ = PF[:, 0:1024]
        pS = [PF[:, 0:256], PF[:, 512:768]]
        pPV = PF[:, 1024:2048]
        pA4 = PF[:, 2048:2304]
        pA4b = PF[:1, 2304:2560]
        pRZ = PF[:, 2048:2050]
        pZ = PF[:1, 3072:3328]
        pG = [PF[:, 0:512], PF[:, 512:1024]]
        pVQ = PF[:, 2048:2560]
        pVU = PF[:, 2560:3072]
        pT = PB[:, 0:1024]
        pT4 = PB[:, 512:1024]
        pMR = PB[:1, 256:512]
        pMR2 = PB[:, 256:384]
        pRZB = PB[:, 256:260]

        def dma_in(src_ap, dst_ap):
            return em.inc("sync", lambda e, s=src_ap, d=dst_ap: e.dma_start(out=d, in_=s), sIN, 16)

        def dma_out(dst_ap, src_ap):
            return em.inc("scalar", lambda e, s=src_ap, d=dst_ap: e.dma_start(out=d, in_=s), sOUT, 16)

        def TT(*args):
            if len(args) == 5:
                args = args[1:]
            out, a, b, op = args
            return lambda e, out=out, a=a, b=b, op=op: e.tensor_tensor(out, a, b, op)

        def TS(out, a, s1, s2, op1, op2=None):
            if s2 is None:
                return lambda e, out=out, a=a, s1=s1, op1=op1: e.tensor_scalar(out, a, s1, None, op1)
            return lambda e, out=out, a=a, s1=s1, s2=s2, op1=op1, op2=op2: e.tensor_scalar(out, a, s1, s2, op1, op2)

        dma_in(IDB.ap(), idb[:, :])
        dma_in(IDF.ap(), idf[:, :])
        v_bio = dma_in(BIO.ap(), bio[:, :])
        dma_in(CST.ap(), cstb[:, :])
        em.on("vector", lambda e: e.memset(onesb[:1, :], 1.0))
        em.on("vector", lambda e: e.memset(onesf[:1, :], 1.0))
        em.on("vector", lambda e: e.memset(onesc[:, :1], 1.0))

        xin3 = xin.rearrange("p (t f) -> p t f", f=1024)
        wcast = 0
        for Wsrc, wdst in ((WQ, p1wq), (WK, p1wk), (WV, p1wv)):
            for g in range(2):
                em.wait("sync", sVE, wcast)
                v = dma_in(
                    bass.AP(Wsrc, g * 4 * 128 * 1024, [[1024, 128], [128 * 1024, 4], [1, 1024]]),
                    xin3,
                )
                em.wait("vector", sIN, v)
                wcast = em.inc(
                    "vector",
                    lambda e, g=g, wdst=wdst: e.tensor_copy(
                        wdst.rearrange("p (t f) -> p t f", f=1024)[:, 4 * g:4 * g + 4, :], xin3),
                    sVE,
                )

        def wsl(wbuf, hc, nch):
            return wbuf.rearrange("p (hc f) -> p hc f", f=1024)[:, hc, 512 * nch:512 * nch + 512]

        xfm3 = xfm.rearrange("p (hc f) -> p hc f", f=512)
        qfmb3 = qfmb.rearrange("p (hc f) -> p hc f", f=512)
        kfm3 = kfm.rearrange("p (hc f) -> p hc f", f=4096)
        pT3 = pT.rearrange("p (hc f) -> p hc f", f=128)
        pe = ptm.rearrange("p (k two) -> p k two", two=2)
        xe = pxx.rearrange("p (k two) -> p k two", two=2)

        xcast_done = wcast
        xt_evac = 0
        qk_evac = 0
        tab_free = {0: 0, 1: 0}
        spillq = 0
        last_mm = 0
        vspill = 0

        def proj_mms(wbuf, tt):
            r = 0
            for nch in range(2):
                for hc in range(8):
                    fn = lambda e, tt=tt, hc=hc, nch=nch, wbuf=wbuf: e.matmul(
                        pQ[:, 512 * nch:512 * nch + 512],
                        xfm[:, 512 * hc + 128 * tt: 512 * hc + 128 * tt + 128],
                        wsl(wbuf, hc, nch),
                        start=(hc == 0), stop=(hc == 7))
                    if nch == 1 and hc == 7:
                        r = em.inc("tensor", fn, sTE)
                    else:
                        em.on("tensor", fn)
            return r

        def xpos_emit(tq0, tq1):
            em.on("vector", TT("vector", pxa[:, :], pe[:, :, 0:1], tq0[:, :], ALU.mult))
            em.on("vector", TT("vector", pxb[:, :], pe[:, :, 1:2], tq1[:, :], ALU.mult))
            em.on("vector", TT("vector", xe[:, :, 0:1], pxa[:, :], pxb[:, :], ALU.subtract))
            em.on("vector", TT("vector", pxa[:, :], pe[:, :, 1:2], tq0[:, :], ALU.mult))
            em.on("vector", TT("vector", pxb[:, :], pe[:, :, 0:1], tq1[:, :], ALU.mult))
            return em.inc("vector", TT("vector", xe[:, :, 1:2], pxa[:, :], pxb[:, :], ALU.add), sVE)

        def tpose8(src, sem_=sTE):
            r = 0
            for hc in range(8):
                fn = lambda e, hc=hc, src=src: e.transpose(
                    pT[:, 128 * hc:128 * hc + 128], src[:, 128 * hc:128 * hc + 128], idb[:, :])
                if hc == 7:
                    r = em.inc("tensor", fn, sem_)
                else:
                    em.on("tensor", fn)
            return r

        for c in range(NCH):
            em.wait("sync", sVE, xcast_done)
            vx = dma_in(
                bass.AP(X, c * 512 * 1024, [[1024, 128], [128 * 1024, 4], [1, 1024]]), xin3)
            em.wait("vector", sIN, vx)
            em.wait("vector", sTE, last_mm)
            xcast_done = em.inc("vector", lambda e: e.tensor_copy(xbf[:, :], xin3), sVE)

            for tt in range(4):
                em.wait("tensor", sVE, xcast_done)
                em.wait("tensor", sVE, qk_evac)
                tp = tpose8(xbf[:, 1024 * tt:1024 * tt + 1024])
                em.wait("vector", sTE, tp)
                xt_evac = em.inc(
                    "vector",
                    lambda e, tt=tt: e.tensor_copy(xfm3[:, :, 128 * tt:128 * tt + 128], pT3),
                    sVE)

                par = tt % 2
                tq = tt0 if par == 0 else tt1
                row0 = c * 512 + tt * 128
                em.wait("sync", sTE, tab_free[par])
                dma_in(bass.AP(TCQ, row0 * 512, [[512, 128], [1, 512]]), tq[0][:, :])
                vtq = dma_in(bass.AP(TSQ, row0 * 512, [[512, 128], [1, 512]]), tq[1][:, :])

                em.wait("tensor", sVE, xt_evac)
                mmv = proj_mms(p1wv, tt)
                em.wait("scalar", sTE, mmv)
                em.wait("scalar", sOUT, vspill)
                ve = em.inc("scalar", lambda e: e.activation(out=ptm[:, :], in_=pQ[:, :], func=AF.Copy), sAC)
                em.wait("scalar", sAC, ve)
                vspill = dma_out(
                    bass.AP(VS, (4 * c + tt) * 128 * 1024, [[1024, 128], [1, 1024]]), ptm[:, :])

                em.wait("tensor", sAC, ve)
                mmq = proj_mms(p1wq, tt)
                em.wait("scalar", sTE, mmq)
                em.wait("scalar", sOUT, vspill)
                qe = em.inc("scalar", lambda e: e.activation(out=ptm[:, :], in_=pQ[:, :], func=AF.Copy), sAC)
                em.wait("vector", sAC, qe)
                em.wait("vector", sIN, vtq)
                xq_done = xpos_emit(tq[0], tq[1])
                em.wait("sync", sVE, xq_done)
                dma_in(bass.AP(TCK, row0 * 512, [[512, 128], [1, 512]]), tq[0][:, :])
                vtk = dma_in(bass.AP(TSK, row0 * 512, [[512, 128], [1, 512]]), tq[1][:, :])
                em.wait("tensor", sVE, xq_done)
                em.wait("tensor", sVE, qk_evac)
                tpq = tpose8(pxx)
                em.wait("vector", sTE, tpq)
                if tt == 0:
                    em.wait("vector", sOUT, spillq)
                qk_evac = em.inc(
                    "vector",
                    lambda e, tt=tt: e.tensor_copy(qfmb3[:, :, 128 * tt:128 * tt + 128], pT3),
                    sVE)

                em.wait("tensor", sAC, qe)
                mmk = proj_mms(p1wk, tt)
                em.wait("scalar", sTE, mmk)
                em.wait("scalar", sVE, xq_done)
                ke = em.inc("scalar", lambda e: e.activation(out=ptm[:, :], in_=pQ[:, :], func=AF.Copy), sAC)
                em.wait("vector", sAC, ke)
                em.wait("vector", sIN, vtk)
                xk_done = xpos_emit(tq[0], tq[1])
                em.wait("tensor", sVE, xk_done)
                em.wait("tensor", sVE, qk_evac)
                tpk = tpose8(pxx)
                tab_free[par] = tpk
                em.wait("vector", sTE, tpk)
                qk_evac = em.inc(
                    "vector",
                    lambda e, c=c, tt=tt: e.tensor_copy(
                        kfm3[:, :, c * 512 + 128 * tt: c * 512 + 128 * tt + 128], pT3),
                    sVE)
                last_mm = tpk

            em.wait("scalar", sVE, qk_evac)
            spillq = dma_out(
                bass.AP(QF, c * 512, [[L, 128], [128 * L, 8], [1, 512]]), qfmb3)

        st_s = {"g1": 0}

        def emit_quant(w, out_bf, src_ready=None):
            nb = w // 16
            if src_ready is not None:
                em.wait("scalar", src_ready[0], src_ready[1])
            em.wait("vector", sGP, st_s["g1"])
            vf = em.inc("vector", lambda e, w=w, nb=nb: e.tensor_reduce(
                qbm[:, :nb], qsrc.rearrange("p (nb k) -> p nb k", k=16)[:, :nb, :],
                AXX, ALU.max, apply_absolute_value=True), sVE)
            em.wait("vector", sVE, vf)
            vf = em.inc("vector", TS(qsc[:, :nb], qbm[:, :nb], 1.0 / 6.0, 1e-12, ALU.mult, ALU.max), sVE)
            em.wait("vector", sVE, vf)
            em.on("vector", TS(qs4[:, :nb], qsc[:, :nb], 0.25, None, ALU.mult))
            vf = em.inc("vector", lambda e, nb=nb: e.reciprocal(qrc[:, :nb], qsc[:, :nb]), sVE)
            em.wait("vector", sVE, vf)
            vsc = em.inc("vector", TT(qrc2[:, :nb], qrc[:, :nb], qrc[:, :nb], ALU.mult), sVE)
            em.wait("vector", sVE, vsc)
            s4b = bass.AP(qs4.tensor, qs4.offset, [[ARF_N, 128], [1, nb], [0, 16]])
            sq = em.inc("scalar", lambda e, w=w: e.activation(
                out=qu[:, :w], in_=qsrc[:, :w], func=AF.Square), sAC)
            rcb = bass.AP(qrc2.tensor, qrc2.offset, [[ARF_N, 128], [1, nb], [0, 16]])
            em.wait("vector", sAC, sq)
            vu = em.inc("vector", TT(qu2[:, :w], qu[:, :w], rcb, ALU.mult), sVE)
            em.wait("scalar", sVE, vu)
            sg = em.inc("scalar", lambda e, w=w: e.activation(
                out=qbSG[:, :w], in_=qsrc[:, :w], func=AF.Sign), sAC)

            def cmp(dst, k):
                return em.inc("scalar", lambda e, dst=dst, k=k, w=w: e.activation(
                    out=dst[:, :w], in_=qu2[:, :w], func=AF.Sign,
                    bias=cstb[:, 62 + k:63 + k]), sAC)

            cmp(qbA, 0)
            a2 = cmp(qbB, 1)
            em.wait("vector", sAC, a2)
            v1 = em.inc("vector", TT(qbS1[:, :w], qbA[:, :w], qbB[:, :w], ALU.add), sVE)
            em.wait("scalar", sVE, v1)
            cmp(qbA, 2)
            a4 = cmp(qbB, 3)
            em.wait("vector", sAC, a4)
            v2 = em.inc("vector", TT(qbS2[:, :w], qbA[:, :w], qbB[:, :w], ALU.add), sVE)
            em.wait("scalar", sVE, v2)
            cmp(qbA, 4)
            a6 = cmp(qbB, 5)
            em.wait("vector", sAC, a6)
            v3 = em.inc("vector", TT(qbS3[:, :w], qbA[:, :w], qbB[:, :w], ALU.add), sVE)
            em.wait("scalar", sVE, v3)
            a7 = cmp(qbA, 6)
            em.on("vector", TT(qbB[:, :w], qbS1[:, :w], qbS2[:, :w], ALU.add))
            em.wait("vector", sAC, a7)
            em.on("vector", lambda e, w=w: e.scalar_tensor_tensor(
                qbS2[:, :w], qbS3[:, :w], 2.0, qbB[:, :w], ALU.mult, ALU.add))
            em.on("vector", lambda e, w=w: e.scalar_tensor_tensor(
                qbS1[:, :w], qbA[:, :w], 4.0, qbS2[:, :w], ALU.mult, ALU.add))
            em.wait("vector", sAC, sg)
            vf1 = em.inc("vector", lambda e, w=w: e.scalar_tensor_tensor(
                qbS3[:, :w], qbS1[:, :w], 12.0, qbSG[:, :w], ALU.add, ALU.mult), sVE)
            em.wait("gpsimd", sVE, vf1)
            g1 = em.inc("gpsimd", TT(out_bf[:, :w], qbS3[:, :w], s4b, ALU.mult), sGP)
            st_s["g1"] = g1
            return g1

        def thb(k, w):
            col = cstb[:, 71 + k:72 + k]
            return bass.AP(col.tensor, col.offset, [[80, 128], [0, w]])

        def emit_quant_v(w, out_bf, bf, st, src_ready=None):
            nb = w // 16
            A, Bb, S1, S2, S3, SG = bf["A"], bf["B"], bf["S1"], bf["S2"], bf["S3"], bf["SG"]
            qup, qu2p = bf["qu"], bf["qu2"]
            bm, sc, s2, rc, rc2 = bf["bm"], bf["sc"], bf["s2"], bf["rc"], bf["rc2"]
            em.wait("vector", sGP, st["g1"])
            vf = em.inc("vector", lambda e, nb=nb: e.tensor_reduce(
                bm[:, :nb], qsrc.rearrange("p (nb k) -> p nb k", k=16)[:, :nb, :],
                AXX, ALU.max, apply_absolute_value=True), sVE)
            em.wait("vector", sVE, vf)
            vf = em.inc("vector", TS(sc[:, :nb], bm[:, :nb], 1.0 / 6.0, 1e-12, ALU.mult, ALU.max), sVE)
            em.wait("vector", sVE, vf)
            em.on("vector", TS(s2[:, :nb], sc[:, :nb], 0.5, None, ALU.mult))
            vf = em.inc("vector", lambda e, nb=nb: e.reciprocal(rc[:, :nb], sc[:, :nb]), sVE)
            em.wait("vector", sVE, vf)
            vf = em.inc("vector", TT(rc2[:, :nb], rc[:, :nb], rc[:, :nb], ALU.mult), sVE)
            em.wait("vector", sVE, vf)
            em.on("vector", TT(qup[:, :w], qsrc[:, :w], qsrc[:, :w], ALU.mult))
            rcb = bass.AP(rc2.tensor, rc2.offset, [[ARF_N, 128], [1, nb], [0, 16]])
            qv = em.inc("vector", TT(qu2p[:, :w], qup[:, :w], rcb, ALU.mult), sVE)
            if src_ready is not None:
                em.wait("scalar", src_ready[0], src_ready[1])
            em.wait("scalar", sVE, st["f1"])
            sgn = em.inc("scalar", lambda e, w=w: e.activation(
                out=SG[:, :w], in_=qsrc[:, :w], func=AF.Sign), sAC)
            em.on("vector", TT(A[:, :w], qu2p[:, :w], thb(0, w), ALU.is_gt))
            em.on("vector", TT(Bb[:, :w], qu2p[:, :w], thb(1, w), ALU.is_gt))
            em.on("vector", TT(S1[:, :w], A[:, :w], Bb[:, :w], ALU.add))
            em.on("vector", TT(A[:, :w], qu2p[:, :w], thb(2, w), ALU.is_gt))
            em.on("vector", TT(Bb[:, :w], qu2p[:, :w], thb(3, w), ALU.is_gt))
            em.on("vector", TT(S2[:, :w], A[:, :w], Bb[:, :w], ALU.add))
            em.on("vector", TT(A[:, :w], qu2p[:, :w], thb(4, w), ALU.is_gt))
            em.on("vector", TT(Bb[:, :w], qu2p[:, :w], thb(5, w), ALU.is_gt))
            em.on("vector", TT(S3[:, :w], A[:, :w], Bb[:, :w], ALU.add))
            em.on("vector", TT(A[:, :w], qu2p[:, :w], thb(6, w), ALU.is_gt))
            em.on("vector", TT(Bb[:, :w], S1[:, :w], S2[:, :w], ALU.add))
            em.on("vector", lambda e, w=w, S3=S3, S1=S1, Bb=Bb: e.scalar_tensor_tensor(
                S1[:, :w], S3[:, :w], 2.0, Bb[:, :w], ALU.mult, ALU.add))
            em.on("vector", lambda e, w=w, A=A, S2=S2, S1=S1: e.scalar_tensor_tensor(
                S2[:, :w], A[:, :w], 4.0, S1[:, :w], ALU.mult, ALU.add))
            em.wait("vector", sAC, sgn)
            f1 = em.inc("vector", TT(S3[:, :w], S2[:, :w], SG[:, :w], ALU.mult), sVE)
            st["f1"] = f1
            s2b = bass.AP(s2.tensor, s2.offset, [[ARF_N, 128], [1, nb], [0, 16]])
            em.wait("gpsimd", sVE, f1)
            g1 = em.inc("gpsimd", TT(out_bf[:, :w], S3[:, :w], s2b, ALU.mult), sGP)
            st["g1"] = g1
            return {"g1": g1, "v": qv, "a": sgn}

        bufs_p2v = {"A": qbA, "B": qbB, "S1": qbS1, "S2": qbS2, "S3": qbS3,
                    "SG": qbSG, "qu": qu, "qu2": qu2,
                    "bm": vbm, "sc": vsc, "s2": vs2, "rc": vrc, "rc2": vrc2}
        bufs_p3v = {"A": vA, "B": vB, "S1": vS1, "S2": vS2, "S3": vS3,
                    "SG": vSG, "qu": pVQ, "qu2": pVU,
                    "bm": vbm, "sc": vsc, "s2": vs2, "rc": vrc, "rc2": vrc2}
        st_p2v = {"f1": 0, "g1": 0}
        st_p3v = {"f1": 0, "g1": 0}

        wq_tiles = []
        for Wsrc, tag, rows, cols in ((WG, "g", 22, 1024), (WU, "u", 22, 1024), (WD, "d", 8, 2816)):
            ccs = [0, 512] if cols == 1024 else [0, 512, 1024, 1536, 2048, 2560]
            for rt in range(rows):
                for c0 in ccs:
                    wq_tiles.append((Wsrc, tag, rt, c0, min(512, cols - c0), cols))

        wq_state = {"busy_v": 0, "busy_a": 0, "busy_o": 0, "busy_t": 0}

        def emit_wq_tile(ti):
            Wsrc, tag, rt, c0, w, cols = wq_tiles[ti]
            em.wait("sync", sVE, wq_state["busy_v"])
            em.wait("sync", sAC, wq_state["busy_a"])
            vw = dma_in(bass.AP(Wsrc, rt * 128 * cols + c0, [[cols, 128], [1, w]]), qsrc[:, :w])
            em.wait("vector", sIN, vw)
            em.wait("vector", sOUT, wq_state["busy_o"])
            ret = emit_quant_v(w, wqo, bufs_p2v, st_p2v, src_ready=(sIN, vw))
            wq_state["busy_v"] = ret["v"]
            wq_state["busy_a"] = ret["a"]
            ntp = w // 128
            em.wait("tensor", sGP, ret["g1"])
            em.wait("tensor", sVE, wq_state["busy_t"])
            r = 0
            for k in range(ntp):
                fn = lambda e, k=k: e.transpose(pT4[:, 128 * k:128 * k + 128], wqo[:, 128 * k:128 * k + 128], idb[:, :])
                if k == ntp - 1:
                    r = em.inc("tensor", fn, sTE)
                else:
                    em.on("tensor", fn)
            em.wait("vector", sTE, r)
            ev = em.inc("vector", lambda e, ntp=ntp: e.tensor_copy(qbA[:, :128 * ntp], pT4[:, :128 * ntp]), sVE)
            wq_state["busy_t"] = ev
            em.wait("scalar", sVE, ev)
            o = 0
            for k in range(ntp):
                blk = (c0 + 128 * k) // 128
                if tag == "g":
                    dst = bass.AP(WGT, blk * 128 * II + rt * 128, [[II, 128], [1, 128]])
                elif tag == "u":
                    dst = bass.AP(WUT, blk * 128 * II + rt * 128, [[II, 128], [1, 128]])
                else:
                    dst = bass.AP(WDT, blk * 128 * H + rt * 128, [[H, 128], [1, 128]])
                o = em.inc("scalar", lambda e, dst=dst, k=k: e.dma_start(
                    out=dst, in_=qbA[:, 128 * k:128 * k + 128]), sOUT, 16)
            wq_state["busy_o"] = o

        wq_cursor = [0]

        def emit_wq_batch(n):
            while n > 0 and wq_cursor[0] < len(wq_tiles):
                emit_wq_tile(wq_cursor[0])
                wq_cursor[0] += 1
                n -= 1

        p1_T = em.val(sTE)
        p1_V = em.val(sVE)
        p1_O = em.val(sOUT)
        em.wait("sync", sTE, p1_T)
        em.wait("sync", sVE, p1_V)
        em.wait("sync", sOUT, p1_O)
        dma_in(bass.AP(VS, 0, [[1024, 128], [128 * 1024, 32], [1, 1024]]),
               vtm.rearrange("p (t f) -> p t f", f=1024))

        ssb3 = ssb.rearrange("p (t f) -> p t f", f=QB)

        def uslot(kt):
            return uextra[:, :] if kt == 0 else ssb3[:, kt - 1, :]
        vtm3 = vtm.rearrange("p (t f) -> p t f", f=1024)
        qblk3 = qblk.rearrange("p (hc f) -> p hc f", f=QB)

        pv_ev = 0
        prev_ze = 0
        prev_evz = 0
        prev_ssb_free = 0
        prev_dg = 0
        last_smm = 0
        for qb in range(NQB):
            em.wait("sync", sTE, last_smm)
            vq = dma_in(bass.AP(QF, qb * QB, [[L, 128], [128 * L, 8], [1, QB]]), qblk3)

            em.wait("tensor", sIN, vq)
            em.wait("tensor", sAC, prev_ze)
            evs = {}
            for kt in range(NKT):
                par = kt % 2
                if kt >= 2:
                    em.wait("tensor", sAC, evs[kt - 2])
                for hc in range(8):
                    fn = lambda e, kt=kt, hc=hc, par=par: e.matmul(
                        pS[par],
                        kfm[:, 4096 * hc + 128 * kt: 4096 * hc + 128 * kt + 128],
                        qblk3[:, hc, :],
                        start=(hc == 0), stop=(hc == 7))
                    if hc == 7:
                        mms = em.inc("tensor", fn, sTE)
                    else:
                        em.on("tensor", fn)
                em.wait("scalar", sTE, mms)
                if kt == 0:
                    em.wait("scalar", sTE, prev_ssb_free)
                evs[kt] = em.inc("scalar", lambda e, kt=kt, par=par: e.activation(
                    out=ssb3[:, kt, :], in_=pS[par], func=AF.Copy), sAC)
            last_smm = mms

            em.wait("vector", sAC, evs[NKT - 1])
            vmx = em.inc("vector", lambda e: e.tensor_reduce(
                mxf[:, :], bass.AP(ssb.tensor, ssb.offset, [[ARB_N, 128], [1, QB], [QB, NKT]]),
                AXX, ALU.max), sVE)
            em.wait("tensor", sVE, vmx)
            em.wait("tensor", sAC, prev_dg)
            for c2 in range(2):
                fn = lambda e, c2=c2: e.transpose(
                    pA4[:, 128 * c2:128 * c2 + 128], mxf[:, 128 * c2:128 * c2 + 128], idf[:, :])
                if c2 == 1:
                    tmx = em.inc("tensor", fn, sTE)
                else:
                    em.on("tensor", fn)
            em.wait("vector", sTE, tmx)
            em.on("vector", lambda e: e.reduce_max(mq4[:, 0:1], pA4[:, 0:128], AXX))
            vf = em.inc("vector", lambda e: e.reduce_max(mq4[:, 1:2], pA4[:, 128:256], AXX), sVE)
            em.wait("vector", sVE, vf)
            em.on("vector", lambda e: e.tensor_copy(mq4b[:, 0:1], mq4[:, 0:1]))
            vmq = em.inc("vector", lambda e: e.tensor_copy(mq4b[:, 1:2], mq4[:, 1:2]), sVE)
            em.wait("tensor", sVE, vmq)
            tmr1 = em.inc("tensor", lambda e: e.transpose(
                pMR2[:1, 0:128], mq4b[:, 0:1], idb[:, :]), sTE)
            em.wait("scalar", sTE, tmr1)
            ev1 = em.inc("scalar", lambda e: e.activation(
                out=mrowB[:1, 0:128], in_=pMR2[:1, 0:128], func=AF.Copy), sAC)
            em.wait("tensor", sAC, ev1)
            tmr2 = em.inc("tensor", lambda e: e.transpose(
                pMR2[:1, 0:128], mq4b[:, 1:2], idb[:, :]), sTE)
            em.wait("scalar", sTE, tmr2)
            evmr = em.inc("scalar", lambda e: e.activation(
                out=mrowB[:1, 128:256], in_=pMR2[:1, 0:128], func=AF.Copy), sAC)
            if DEBUG_SCRATCH and qb == 0:
                em.wait("scalar", sAC, evmr)
                dma_out(DM2.ap(), mq4)
                dma_out(DM3.ap(), mrowB[:1, :])
            em.wait("tensor", sAC, evmr)
            tmb = em.inc("tensor", lambda e: e.matmul(
                pA4, onesb[:1, :], mrowB[:1, :], start=True, stop=True), sTE)
            em.wait("scalar", sTE, tmb)
            evmb = em.inc("scalar", lambda e: e.activation(out=mbc[:, :], in_=pA4, func=AF.Copy), sAC)
            if DEBUG_SCRATCH and qb == 0:
                em.wait("scalar", sAC, evmb)
                dma_out(DS.ap(), ssb)
                dma_out(DM.ap(), mbc)
                dma_out(DM4.ap(), mxf)

            zmm = {}
            ud = 0
            for kt in range(NKT):
                par = kt % 2
                if kt < 2:
                    em.wait("vector", sAC, evmb)
                else:
                    em.wait("vector", sAC, evs_exp[kt - 2])
                vsub = em.inc("vector", lambda e, kt=kt, par=par: e.tensor_tensor(
                    pS[par], ssb3[:, kt, :], mbc[:, :], ALU.subtract), sVE)
                em.wait("scalar", sVE, vsub)
                if kt == 0:
                    evs_exp = {}
                evs_exp[kt] = em.inc("scalar", lambda e, kt=kt, par=par: e.activation(
                    out=ssb3[:, kt, :], in_=pS[par], func=AF.Exp), sAC)
                oidx = kt - 2 * qb + 30
                if kt >= 2 or qb > 0:
                    em.wait("scalar", sVE, ud)
                em.on("scalar", lambda e, par=par, oidx=oidx: e.activation(
                    out=pA4, in_=bio[:, 0:QB], func=AF.Abs, bias=cstb[:, oidx:oidx + 1]))
                dg = em.inc("scalar", lambda e, par=par: e.activation(
                    out=dtb[par][:, :], in_=pA4, func=AF.Exp, scale=cstb[:, 69:70]), sAC)
                em.wait("tensor", sAC, evs_exp[kt])
                if kt == 0:
                    em.wait("tensor", sAC, prev_evz)
                zmm[kt] = em.inc("tensor", lambda e, kt=kt: e.matmul(
                    pZ, onesc[:, :1], ssb3[:, kt, :],
                    start=(kt == 0), stop=(kt == NKT - 1)), sTE)
                em.wait("vector", sAC, dg)
                em.wait("vector", sTE, zmm[kt])
                ud = em.inc("vector", lambda e, kt=kt, par=par: e.tensor_tensor(
                    uslot(kt), ssb3[:, kt, :], dtb[par][:, :], ALU.mult), sVE)
            prev_ze = evs_exp[NKT - 1]
            prev_dg = dg

            em.wait("scalar", sTE, zmm[NKT - 1])
            evz = em.inc("scalar", lambda e: e.activation(out=zrow[:1, :], in_=pZ, func=AF.Copy), sAC)
            prev_evz = evz
            em.wait("vector", sAC, evz)
            vf = em.inc("vector", lambda e: e.reciprocal(rzf[:1, :], zrow[:1, :]), sVE)
            em.wait("vector", sVE, vf)
            vrz = em.inc("vector", lambda e: e.tensor_copy(rzb[:1, :], rzf[:1, :]), sVE)
            em.wait("tensor", sVE, vrz)
            trz1 = em.inc("tensor", lambda e: e.transpose(
                pRZB[:, 0:1], rzb[:1, 0:128], onesb[:1, :1]), sTE)
            em.wait("scalar", sTE, trz1)
            evr1 = em.inc("scalar", lambda e: e.activation(
                out=rzc[:, 0:1], in_=pRZB[:, 0:1], func=AF.Copy), sAC)
            em.wait("tensor", sAC, evr1)
            trz2 = em.inc("tensor", lambda e: e.transpose(
                pRZB[:, 0:1], rzb[:1, 128:256], onesb[:1, :1]), sTE)
            em.wait("scalar", sTE, trz2)
            evrz = em.inc("scalar", lambda e: e.activation(
                out=rzc[:, 1:2], in_=pRZB[:, 0:1], func=AF.Copy), sAC)

            if DEBUG_SCRATCH and qb == 0:
                em.wait("scalar", sVE, ud)
                em.wait("scalar", sAC, evrz)
                dma_out(DM5.ap(), ssb)
                dma_out(DM6.ap(), uextra)
                dma_out(DM7.ap(), zrow[:1, :])
                dma_out(DM8.ap(), rzc)
            em.wait("tensor", sVE, ud)
            for qs in range(2):
                em.wait("sync", sVE, pv_ev)
                em.wait("sync", sOUT, em.val(sOUT))
                vxr = dma_in(bass.AP(X, (qb * QB + qs * 128) * 1024, [[1024, 128], [1, 1024]]), xres[:, :])
                em.wait("tensor", sVE, pv_ev)
                for kt in range(NKT):
                    for nv in range(2):
                        fn = lambda e, kt=kt, qs=qs, nv=nv: e.matmul(
                            pPV[:, 512 * nv:512 * nv + 512],
                            uslot(kt)[:, 128 * qs:128 * qs + 128],
                            vtm3[:, kt, 512 * nv:512 * nv + 512],
                            start=(kt == 0), stop=(kt == NKT - 1))
                        if kt == NKT - 1 and nv == 1:
                            mpv = em.inc("tensor", fn, sTE)
                        else:
                            em.on("tensor", fn)
                em.wait("vector", sTE, mpv)
                em.wait("vector", sAC, evrz)
                em.wait("vector", sIN, vxr)
                em.on("vector", lambda e, qs=qs: e.scalar_tensor_tensor(
                    ypre[:, :], pPV[:, :], rzc[:, qs:qs + 1], xres[:, :], ALU.mult, ALU.add))
                vf = em.inc("vector", lambda e: e.reduce_sum(stat[:, 0:1], ypre[:, :], AXX), sVE)
                em.wait("vector", sVE, vf)
                vf = em.inc("vector", TS(stat[:, 1:2], stat[:, 0:1], 1.0 / 1024.0, None, ALU.mult), sVE)
                em.wait("vector", sVE, vf)
                em.on("vector", lambda e: e.tensor_scalar(ypc[:, :], ypre[:, :], stat[:, 1:2], None, ALU.subtract))
                vvar = em.inc("vector", lambda e: e.scalar_tensor_tensor(
                    xres[:, :], ypc[:, :], 1.0, ypc[:, :], ALU.bypass, ALU.mult,
                    accum_out=stat[:, 2:3]), sVE)
                pv_ev = vvar
                em.wait("vector", sVE, vvar)
                vpe = em.inc("vector", TS(stat[:, 4:5], stat[:, 2:3], 1.0 / 1024.0, EPS, ALU.mult, ALU.add), sVE)
                em.wait("scalar", sVE, vpe)
                em.on("scalar", lambda e: e.activation(out=stat[:, 5:6], in_=stat[:, 4:5], func=AF.Ln))
                rstd = em.inc("scalar", lambda e: e.activation(
                    out=stat[:, 6:7], in_=stat[:, 5:6], func=AF.Exp, scale=cstb[:, 70:71]), sAC)
                em.wait("vector", sAC, rstd)
                yv = em.inc("vector", lambda e: e.tensor_scalar(
                    xres[:, :], ypc[:, :], stat[:, 6:7], None, ALU.mult), sVE)
                em.wait("scalar", sVE, yv)
                if DEBUG_SCRATCH and qb == 0 and qs == 0:
                    dma_out(DM9.ap(), ypre)
                    dma_out(DM10.ap(), stat)
                dma_out(
                    bass.AP(YS, (qb * QB + qs * 128) * 1024, [[1024, 128], [1, 1024]]), xres[:, :])
            prev_ssb_free = mpv
            emit_wq_batch(9)
        emit_wq_batch(len(wq_tiles))

        p2_T = em.val(sTE)
        p2_V = em.val(sVE)
        p2_O = em.val(sOUT)
        em.wait("sync", sTE, p2_T)
        em.wait("sync", sVE, p2_V)
        em.wait("sync", sOUT, p2_O)
        em.wait("vector", sGP, em.val(sGP))
        dma_in(bass.AP(WGT, 0, [[II, 128], [128 * II, 8], [1, II]]),
               wgTr.rearrange("p (hc f) -> p hc f", f=II))
        dma_in(bass.AP(WUT, 0, [[II, 128], [128 * II, 8], [1, II]]),
               wuTr.rearrange("p (hc f) -> p hc f", f=II))
        vwl = dma_in(bass.AP(WDT, 0, [[H, 128], [128 * H, 22], [1, H]]),
                     wdTr.rearrange("p (ic f) -> p ic f", f=H))

        wgTr3 = wgTr.rearrange("p (hc f) -> p hc f", f=II)
        wuTr3 = wuTr.rearrange("p (hc f) -> p hc f", f=II)
        wdTr3 = wdTr.rearrange("p (ic f) -> p ic f", f=H)
        ICH = [(0, 512), (512, 512), (1024, 512), (1536, 512), (2048, 512), (2560, 256)]

        p3qc = [0]

        def p3_quant(w, out_bf, src_ready):
            i = p3qc[0]
            p3qc[0] += 1
            if i % 3 == 0:
                ret = emit_quant_v(w, out_bf, bufs_p3v, st_p3v, src_ready=src_ready)
                return (sGP, ret["g1"])
            g1 = emit_quant(w, out_bf, src_ready=src_ready)
            return (sGP, g1)

        yload = 0
        out_done = 0
        prev_yq_T = 0
        prev_hq_T = 0
        pT_ev = 0
        yqT_ev = 0
        hms_prev = {}
        yti_free = 0
        for t in range(32):
            em.wait("sync", sVE, yti_free)
            vy = dma_in(bass.AP(YS, t * 128 * 1024, [[1024, 128], [1, 1024]]), yti[:, :])
            em.wait("vector", sIN, vy)
            yreadies = []
            for yc in range(2):
                em.wait("vector", sTE, prev_yq_T)
                em.wait("gpsimd", sTE, prev_yq_T)
                ycp = em.inc("vector", lambda e, yc=yc: e.tensor_copy(
                    qsrc[:, :], yti[:, 512 * yc:512 * yc + 512]), sVE)
                yreadies.append(p3_quant(512, yqb[:, 512 * yc:512 * yc + 512], (sVE, ycp)))
            for sm, vv_ in yreadies:
                em.wait("tensor", sm, vv_)
            em.wait("tensor", sVE, pT_ev)
            if t == 0:
                em.wait("tensor", sIN, vwl)
            r = 0
            for hc in range(8):
                fn = lambda e, hc=hc: e.transpose(
                    pT[:, 128 * hc:128 * hc + 128], yqb[:, 128 * hc:128 * hc + 128], idb[:, :])
                if hc == 7:
                    r = em.inc("tensor", fn, sTE)
                else:
                    em.on("tensor", fn)
            prev_yq_T = r
            em.wait("vector", sTE, r)
            yqT_ev = em.inc("vector", lambda e: e.tensor_copy(yqT[:, :], pT[:, :]), sVE)
            pT_ev = yqT_ev

            for ci, (c0, w) in enumerate(ICH):
                par = ci % 2
                ntp = w // 128
                if ci >= 2:
                    em.wait("tensor", sVE, hms_prev[ci - 2])
                em.wait("tensor", sVE, yqT_ev)
                for hc in range(8):
                    fn = lambda e, hc=hc, c0=c0, w=w, par=par: e.matmul(
                        pG[par][:, 0:w], yqT[:, 128 * hc:128 * hc + 128],
                        wgTr3[:, hc, c0:c0 + w], start=(hc == 0), stop=(hc == 7))
                    if hc == 7:
                        mg = em.inc("tensor", fn, sTE)
                    else:
                        em.on("tensor", fn)
                em.wait("scalar", sTE, mg)
                em.wait("scalar", sVE, hms_prev.get(("tile", ci), 0))
                gev = em.inc("scalar", lambda e, c0=c0, w=w, par=par: e.activation(
                    out=gs[:, c0:c0 + w], in_=pG[par][:, 0:w], func=AF.Silu), sAC)
                em.wait("tensor", sAC, gev)
                for hc in range(8):
                    fn = lambda e, hc=hc, c0=c0, w=w, par=par: e.matmul(
                        pG[par][:, 0:w], yqT[:, 128 * hc:128 * hc + 128],
                        wuTr3[:, hc, c0:c0 + w], start=(hc == 0), stop=(hc == 7))
                    if hc == 7:
                        mu = em.inc("tensor", fn, sTE)
                    else:
                        em.on("tensor", fn)
                em.wait("vector", sTE, mu)
                hm = em.inc("vector", lambda e, c0=c0, w=w, par=par: e.tensor_tensor(
                    qsrc[:, :w], pG[par][:, 0:w], gs[:, c0:c0 + w], ALU.mult), sVE)
                hms_prev[ci] = hm
                hms_prev[("tile", ci)] = hm
                em.wait("vector", sTE, prev_hq_T)
                em.wait("gpsimd", sTE, prev_hq_T)
                hready = p3_quant(w, wqo, (sVE, hm))
                em.wait("tensor", hready[0], hready[1])
                em.wait("tensor", sVE, pT_ev)
                r2 = 0
                for k in range(ntp):
                    fn = lambda e, k=k: e.transpose(
                        pT[:, 128 * k:128 * k + 128], wqo[:, 128 * k:128 * k + 128], idb[:, :])
                    if k == ntp - 1:
                        r2 = em.inc("tensor", fn, sTE)
                    else:
                        em.on("tensor", fn)
                prev_hq_T = r2
                em.wait("vector", sTE, r2)
                pT_ev = em.inc("vector", lambda e, c0=c0, ntp=ntp: e.tensor_copy(
                    hqT[:, c0:c0 + 128 * ntp], pT[:, :128 * ntp]), sVE)

            em.wait("tensor", sVE, pT_ev)
            em.wait("tensor", sVE, yload)
            for nd in range(2):
                for ic in range(22):
                    fn = lambda e, ic=ic, nd=nd: e.matmul(
                        pPV[:, 512 * nd:512 * nd + 512], hqT[:, 128 * ic:128 * ic + 128],
                        wdTr3[:, ic, 512 * nd:512 * nd + 512],
                        start=(ic == 0), stop=(ic == 21))
                    if ic == 21 and nd == 1:
                        md = em.inc("tensor", fn, sTE)
                    else:
                        em.on("tensor", fn)
            em.wait("vector", sTE, md)
            em.wait("vector", sOUT, out_done)
            yload = em.inc("vector", lambda e: e.scalar_tensor_tensor(
                ypre2[:, :], pPV[:, :], 1.0, yti[:, :], ALU.bypass, ALU.add), sVE)
            yti_free = yload
            vf = em.inc("vector", lambda e: e.reduce_sum(st3[:, 0:1], ypre2[:, :], AXX), sVE)
            em.wait("vector", sVE, vf)
            vf = em.inc("vector", TS(st3[:, 1:2], st3[:, 0:1], 1.0 / 1024.0, None, ALU.mult), sVE)
            em.wait("vector", sVE, vf)
            em.on("vector", lambda e: e.tensor_scalar(gs[:, 0:1024], ypre2[:, :], st3[:, 1:2], None, ALU.subtract))
            vv = em.inc("vector", lambda e: e.scalar_tensor_tensor(
                ypre2[:, :], gs[:, 0:1024], 1.0, gs[:, 0:1024], ALU.bypass, ALU.mult,
                accum_out=st3[:, 2:3]), sVE)
            em.wait("vector", sVE, vv)
            vpe3 = em.inc("vector", TS(st3[:, 4:5], st3[:, 2:3], 1.0 / 1024.0, EPS, ALU.mult, ALU.add), sVE)
            em.wait("scalar", sVE, vpe3)
            em.on("scalar", lambda e: e.activation(out=st3[:, 5:6], in_=st3[:, 4:5], func=AF.Ln))
            rstd3 = em.inc("scalar", lambda e: e.activation(
                out=st3[:, 6:7], in_=st3[:, 5:6], func=AF.Exp, scale=cstb[:, 70:71]), sAC)
            em.wait("vector", sAC, rstd3)
            if DEBUG_SCRATCH and t == 2:
                em.wait("sync", sAC, rstd3)
                dma_out(DM11.ap(), gs[:, 0:1024])
                dma_out(DM12.ap(), st3[:, 0:16])
            yout = em.inc("vector", lambda e: e.tensor_scalar(
                ypre2[:, :], gs[:, 0:1024], st3[:, 6:7], None, ALU.mult), sVE)
            hms_prev[("tile", 0)] = yout
            hms_prev[("tile", 1)] = yout
            em.wait("scalar", sVE, yout)
            out_done = dma_out(
                bass.AP(OUT, t * 128 * 1024, [[1024, 128], [1, 1024]]), ypre2[:, :])

        em.wait("gpsimd", sOUT, em.val(sOUT))
        em.on("gpsimd", lambda e: e.nop())

        block = ctx.enter_context(nc.Block())

        @block.sync
        def _(e):
            for fn in em.q["sync"]:
                fn(e)

        @block.tensor
        def _(e):
            for fn in em.q["tensor"]:
                fn(e)

        @block.vector
        def _(e):
            for fn in em.q["vector"]:
                fn(e)

        @block.scalar
        def _(e):
            for fn in em.q["scalar"]:
                fn(e)

        @block.gpsimd
        def _(e):
            for fn in em.q["gpsimd"]:
                fn(e)

    return nc


def _mk_cst():
    cst = np.zeros((128, 80), dtype=np.float32)
    for j in range(62):
        cst[:, j] = 128.0 * (j - 30)
    for k in range(7):
        cst[:, 62 + k] = -TH2[k]
    cst[:, 69] = LNG
    cst[:, 70] = -0.5
    for k in range(7):
        cst[:, 71 + k] = TH2[k]
    return cst


def kernel(**inputs):
    Xall = np.ascontiguousarray(inputs["L0_attn_X"], dtype=np.float32)
    tcq, tsq, tck, tsk = _mk_tables()
    shared = {
        "wq_in": np.ascontiguousarray(inputs["L0_attn_W_Q"], dtype=np.float32),
        "wk_in": np.ascontiguousarray(inputs["L0_attn_W_K"], dtype=np.float32),
        "wv_in": np.ascontiguousarray(inputs["L0_attn_W_V"], dtype=np.float32),
        "wg_in": np.ascontiguousarray(inputs["L0_ffn_w_gate"], dtype=np.float32),
        "wu_in": np.ascontiguousarray(inputs["L0_ffn_w_up"], dtype=np.float32),
        "wd_in": np.ascontiguousarray(inputs["L0_ffn_w_down"], dtype=np.float32),
        "tcq_in": tcq, "tsq_in": tsq, "tck_in": tck, "tsk_in": tsk,
        "cst_in": _mk_cst(),
        "idb_in": np.eye(128, dtype=bfdt),
        "idf_in": np.eye(128, dtype=np.float32),
        "bio_in": (np.arange(128, dtype=np.float32)[:, None]
                   - np.arange(512, dtype=np.float32)[None, :]),
    }
    nc = build_nc()
    in_maps = [dict(shared, x_in=np.ascontiguousarray(Xall[c])) for c in range(NCORES)]
    res = run_bass_kernel_spmd(nc, in_maps, core_ids=list(range(NCORES)))
    return np.stack([res.results[c]["out"] for c in range(NCORES)], axis=0)
